# version 02
# baseline (speedup 1.0000x reference)
import math
import contextlib
import numpy as np
import ml_dtypes
import concourse.bass as bass
import concourse.mybir as mybir
from concourse.bass_utils import run_bass_kernel_spmd

F32 = mybir.dt.float32
BF16 = mybir.dt.bfloat16
FP8 = mybir.dt.float8e4
AF = mybir.ActivationFunctionType
ALU = mybir.AluOpType
AX = mybir.AxisListType

B, L, D = 4, 2048, 2048
NU, TC, DC, LC = 8, 16, 16, 8
N_CORES = 8

BASE_LR_INV = float(np.float32(0.01 + math.log(-math.expm1(-0.01))))
NS_COEFFS = [(4.0848, -6.8946, 2.927), (3.9505, -6.3029, 2.6377),
             (3.7418, -5.5913, 2.3037), (2.8769, -3.1427, 1.2046),
             (2.8366, -3.0525, 1.2012)]

ENGINES = ("tensor", "vector", "scalar", "sync", "gpsimd")
f32 = np.float32

DEBUG = set()
BARRIERS = False


class Sched:
    def __init__(self):
        self.q = {e: [] for e in ENGINES}
        self.counts = {}
        self.last_wait = {e: {} for e in ENGINES}
        self.semnames = []

    def sem(self, name):
        self.semnames.append(name)
        self.counts[name] = 0

    def op(self, eng, fn, deps=(), inc=None, inc_by=1):
        waits = []
        for t in deps:
            if t is None:
                continue
            peng, s, v = t
            if peng == eng:
                continue
            lw = self.last_wait[eng].get(s, 0)
            if v > lw:
                waits.append((s, v))
                self.last_wait[eng][s] = v
        tick = None
        if inc is not None:
            self.counts[inc] += inc_by
            tick = (eng, inc, self.counts[inc])
        self.q[eng].append((waits, fn, inc, inc_by))
        return tick

    def dma(self, eng, fn, deps=()):
        sem = "dg" if eng == "gpsimd" else "ds"
        t = self.op(eng, fn, deps, inc=sem, inc_by=16)
        return (None, t[1], t[2])

    def hard(self, tick):
        return None if tick is None else (None, tick[1], tick[2])

    def barrier(self):
        last = {}
        for e in ENGINES:
            for i in range(len(self.q[e]) - 1, -1, -1):
                waits, fn, inc, inc_by = self.q[e][i]
                if inc is not None:
                    last[e] = (None, inc, self.counts[inc])
                    break
        deps = list(last.values())
        outs = {}
        for e in ENGINES:
            if e in ("tensor", "vector", "scalar"):
                outs[e] = self.op(e, lambda h: h.nop(), deps, inc={"tensor": "tk",
                                  "vector": "vk", "scalar": "ak"}[e])
        ticks = [self.hard(t) for t in outs.values()]
        for e in ENGINES:
            self.op(e, lambda h: None, ticks)
        return ticks

    def play(self, eng, h, sems):
        for waits, fn, inc, inc_by in self.q[eng]:
            for s, v in waits:
                h.wait_ge(sems[s], v)
            r = fn(h)
            if inc is not None:
                if r is None:
                    r = h.nop()
                r.then_inc(sems[inc], inc_by)


class Banks:
    def __init__(self, ps):
        self.ps = ps
        self.wr = [None] * 8
        self.rd = [[] for _ in range(8)]

    def sl(self, b, off=0, n=512, dtype=F32):
        ap = self.ps[:, 512 * b: 512 * (b + 1)]
        if dtype != F32:
            ap = ap.bitcast(dtype)
        return ap[:, off:off + n]

    def multi(self, b0, nb, off=0, n=512):
        ap = self.ps[:, 512 * b0: 512 * (b0 + nb)]
        return ap.rearrange("p (b c) -> p b c", c=512)[:, :, off:off + n]

    def acquire(self, b):
        deps = list(self.rd[b])
        if self.wr[b] is not None:
            deps.append(self.wr[b])
        self.rd[b] = []
        return deps

    def acq(self, bs):
        deps = []
        for b in bs:
            deps += self.acquire(b)
        return deps

    def wrote(self, b, tick):
        self.wr[b] = tick

    def read(self, b, tick):
        self.rd[b].append(tick)


_NC_CACHE = [None]


def build_program():
    nc = bass.Bass("TRN2", target_bir_lowering=False, debug=False)

    xT_h = nc.dram_tensor("xT", [D, L], F32, kind="ExternalInput").ap()
    xres_h = nc.dram_tensor("xres", [L // 2, D], F32, kind="ExternalInput").ap()
    wcat_h = nc.dram_tensor("wcat", [D, 3096], F32, kind="ExternalInput").ap()
    wo_h = nc.dram_tensor("wo", [D, D], F32, kind="ExternalInput").ap()
    w012t_h = nc.dram_tensor("w012t", [NU * 128, 384], F32, kind="ExternalInput").ap()
    w012n_h = nc.dram_tensor("w012n", [NU * 128, 384], F32, kind="ExternalInput").ap()
    osc_h = nc.dram_tensor("osc", [128, 1], F32, kind="ExternalInput").ap()
    fwT_h = nc.dram_tensor("fwT", [D, D], BF16, kind="ExternalInput").ap()
    fbias_h = nc.dram_tensor("fbias", [128, D], BF16, kind="ExternalInput").ap()
    ident_h = nc.dram_tensor("identm", [128, 128], BF16, kind="ExternalInput").ap()
    out_h = nc.dram_tensor("out", [L // 2, D], F32, kind="ExternalOutput").ap()

    _DBG_SPECS = {
        "qkv": ([128, TC * 3096], BF16),
        "inv": ([128, TC * 16], F32),
        "lr": ([128, TC * 24], F32),
        "a123": ([128, TC * 384], BF16),
        "rhs": ([128, TC * 256], BF16),
        "rhs1": ([128, TC * 128], BF16),
        "dwP": ([128, 24 * 128], BF16),
        "dwO": ([128, 24 * 128], BF16),
        "nsX": ([128, 24 * 128], BF16),
        "wfin": ([128, 24 * 128], BF16),
        "s_all": ([128, 24], F32),
        "rn_raw": ([128, 24], F32),
        "wn0": ([128, 24], F32),
        "ofm": ([128, TC * 1024], BF16),
        "opart": ([128, NU * 1024], BF16),
        "x2": ([128, TC * 1024], BF16),
        "r1c6": ([128, 2048], F32),
        "lnv6": ([128, 8], F32),
        "canary": ([128, 48], F32),
        "xmloc": ([128, 1], F32),
        "ybf4": ([128, 2048], BF16),
        "r2c4": ([128, 2048], F32),
        "scales": ([1, 6], F32),
    }
    dbg = {}
    for name in DEBUG:
        shape, dt = _DBG_SPECS[name]
        dbg[name] = nc.dram_tensor("dbg_" + name, shape, dt, kind="ExternalOutput").ap()

    ag_in = nc.dram_tensor("ag_in", [NU * 128, 1024], BF16)
    ag_out = nc.dram_tensor("ag_out", [2 * NU * 128, 1024], BF16)
    ar_in = nc.dram_tensor("ar_in", [1, 1], F32)
    ar_out = nc.dram_tensor("ar_out", [1, 1], F32)
    col_d = nc.dram_tensor("col_d", [128], F32)
    col_d2 = nc.dram_tensor("col_d2", [128], F32)
    nrow_d = nc.dram_tensor("nrow_d", [NU, 2048], F32)
    rrow_d = nc.dram_tensor("rrow_d", [NU, 2048], BF16)

    S = Sched()
    for s in ("dg", "ds", "tk", "vk", "ak", "cc"):
        S.sem(s)

    with contextlib.ExitStack() as ST:
        def sbuf(name, shape, dt):
            return ST.enter_context(nc.sbuf_tensor(name, shape, dt))

        A_qkv = sbuf("A_qkv", [128, TC * 3096], BF16)
        A_w = sbuf("A_w", [128, 16 * 2048], BF16)
        A_s = sbuf("A_s", [128, 18432], BF16)
        ident = sbuf("identb", [128, 128], BF16)
        ones_col = sbuf("ones_col", [128, 1], BF16)
        ones_row = sbuf("ones_row", [1, 128], BF16)
        ones_row_f = sbuf("ones_row_f", [1, 128], F32)
        osc_s = sbuf("osc_s", [128, 1], F32)
        inv_all = sbuf("inv_all", [128, TC * 16], F32)
        ssum_all = sbuf("ssum_all", [128, TC * 16], F32)
        lr_all = sbuf("lr_all", [128, TC * 24], F32)
        frob_row = sbuf("frob_row", [1, 24], F32)
        invf_col = sbuf("invf_col", [128, 24], F32)
        wnorm0 = sbuf("wnorm0", [128, 24], F32)
        rn_all = sbuf("rn_all", [128, 24], F32)
        s_all = sbuf("s_all", [128, 24], F32)
        s1c = sbuf("s1c", [128, NU], F32)
        mv = sbuf("mv", [128, 4], F32)
        bnst = sbuf("bnst", [128, 24], F32)
        canary1 = sbuf("canary1", [128, 16], F32)
        isig = sbuf("isig", [128, 2], F32)
        msq = sbuf("msq", [128, 2], F32)
        canary2 = sbuf("canary2", [128, 16], F32)
        nbias = sbuf("nbias", [128, 2], F32)
        canary3 = sbuf("canary3", [128, 16], F32)
        xmax = sbuf("xmax", [128, 1], F32)
        xmax8 = sbuf("xmax8", [128, 8], F32)
        wmax16 = sbuf("wmax16", [128, 16], F32)
        wmax = sbuf("wmax", [128, 1], F32)
        tmpc = sbuf("tmpc", [128, 2], F32)
        tmpcb = sbuf("tmpcb", [128, 1], BF16)
        tmpcb24 = sbuf("tmpcb24", [128, 24], BF16)
        row128 = sbuf("row128", [1, 128], F32)
        row128b = sbuf("row128b", [1, 128], F32)
        sc11 = sbuf("sc11", [1, 6], F32)
        xsinv_c = sbuf("xsinv_c", [128, 1], F32)
        dbg_lnv_stage = sbuf("dbg_lnv", [128, 8], F32)
        wsinv_c = sbuf("wsinv_c", [128, 1], F32)
        xw_c = sbuf("xw_c", [128, 1], F32)
        ws_c = sbuf("ws_c", [128, 1], F32)
        xs_c = sbuf("xs_c", [128, 1], F32)
        PS = ST.enter_context(nc.psum_tensor("PS", [128, 4096], F32))
        pk = Banks(PS)

        qkv3 = A_qkv.ap().rearrange("p (t c) -> p t c", c=3096)

        def q_sl(u, tc):
            return qkv3[:, tc, u * 128:(u + 1) * 128]

        def k_sl(u, tc):
            return qkv3[:, tc, 1024 + u * 128:1024 + (u + 1) * 128]

        def v_sl(u, tc):
            return qkv3[:, tc, 2048 + u * 128:2048 + (u + 1) * 128]

        ofm_sl = q_sl

        def ofm_v(u):
            return qkv3[:, :, u * 128:(u + 1) * 128]

        def qfm_half(u, half):
            return qkv3[:, half * 8:(half + 1) * 8, 2048 + u * 128:2048 + (u + 1) * 128]

        def qfm_mov(u, tc0, ntc):
            return qkv3[:, tc0:tc0 + ntc, 2048 + u * 128:2048 + (u + 1) * 128]

        def x2_sl(lc):
            return qkv3[:, 2 * lc:2 * lc + 2, 1024:2048]

        def x2_tile(lc, dcl):
            h, c = divmod(dcl, 8)
            return qkv3[:, 2 * lc + h, 1024 + c * 128:1024 + (c + 1) * 128]

        def x2T_tile(dc, lc):
            if dc < 8:
                return qkv3[:, lc, dc * 128:(dc + 1) * 128]
            return qkv3[:, 8 + lc, (dc - 8) * 128:(dc - 8 + 1) * 128]

        def x2T_chunk(dc):
            if dc < 8:
                return qkv3[:, 0:8, dc * 128:(dc + 1) * 128]
            return qkv3[:, 8:16, (dc - 8) * 128:(dc - 8 + 1) * 128]

        qkvf8 = A_qkv.ap().bitcast(FP8)

        def wq_sl(kc, a, n):
            base = kc * 6192 + 4096
            return qkvf8[:, base + a: base + a + n]

        opart_3d = qkv3[:, 0:8, 2048:3072]
        c1_3d = qkv3[:, 0:8, 1024:2048]
        mine_3d = qkv3[:, 8:16, 0:1024]
        opart4 = qkv3[:, 0:8, 2048:3072].rearrange("p l (j t) -> p l j t", t=128)
        c1_4 = qkv3[:, 0:8, 1024:2048].rearrange("p l (j t) -> p l j t", t=128)

        def opart_sl(j, lc):
            return qkv3[:, lc, 2048 + j * 128:2048 + (j + 1) * 128]

        awb = A_w.ap()

        def aw_b(off_b, n, dt=BF16):
            ap = awb if dt == BF16 else awb.bitcast(dt)
            esz = {BF16: 2, F32: 4, FP8: 1}[dt]
            return ap[:, off_b // esz: off_b // esz + n]

        W_OFF = {"dwP": 0, "dwO": 6144, "nsX": 12288, "nsY": 18432,
                 "nsA": 24576, "nsB": 30720, "w012t": 36864, "w012n": 43008}

        def wmat(nm, m, n=128):
            return aw_b(W_OFF[nm] + m * 256, n)

        def wmat3(nm, u, n=384):
            return aw_b(W_OFF[nm] + u * 768, n)

        def fw_ch(kc):
            return A_w[:, kc * 2048:(kc + 1) * 2048]

        asb = A_s.ap()

        def as_b(off_b, n, dt=BF16):
            ap = asb if dt == BF16 else asb.bitcast(dt)
            esz = {BF16: 2, F32: 4, FP8: 1}[dt]
            return ap[:, off_b // esz: off_b // esz + n]

        def DBG_dump(name, src_ap, deps):
            if name in DEBUG:
                def f(h, src=src_ap):
                    return h.dma_start(out=dbg[name][:, :], in_=src)
                return S.dma("sync", f, deps)
            return None

        t_ident = S.dma("gpsimd", lambda h: h.dma_start(out=ident[:, :], in_=ident_h[:, :]))
        t_osc = S.dma("gpsimd", lambda h: h.dma_start(out=osc_s[:, :], in_=osc_h[:, :]))
        S.op("vector", lambda h: h.memset(ones_col[:, :], 1.0))
        S.op("vector", lambda h: h.memset(ones_row[:, :], 1.0))
        S.op("vector", lambda h: h.memset(ones_row_f[:, :], 1.0), inc="vk")
        S.op("vector", lambda h: h.memset(canary1[:, :], 123.0))
        S.op("vector", lambda h: h.memset(canary2[:, :], 123.0))
        S.op("vector", lambda h: h.memset(canary3[:, :], 123.0), inc="vk")

        xt = [as_b(0, 2048), as_b(4096, 2048)]
        sqs = as_b(8192, 2048)
        wsq = as_b(32768, 2048)

        passes = [(0, 2048, [(0, 512), (512, 1024), (1024, 1536), (1536, 2048)]),
                  (2048, 3096, [(2048, 2560), (2560, 3072), (3072, 3096)])]

        wload = [[None] * DC, [None] * DC]
        for dc in range(DC):
            def f(h, dc=dc):
                return h.dma_start(out=A_w[:, dc * 2048:(dc + 1) * 2048],
                                   in_=wcat_h[dc * 128:(dc + 1) * 128, 0:2048])
            wload[0][dc] = S.dma("gpsimd", f)

        roti = [0]

        def next_bank(banks=(0, 1, 2, 3, 4, 5)):
            b = banks[roti[0] % len(banks)]
            roti[0] += 1
            return b

        xt_load = [None, None]
        xt_guard = [None, None]
        mm_last = None
        pass0_last = None
        t_sqs_free = None
        v_evac_last = None
        lr_evac_last = None

        for pi, (c0, c1, cts) in enumerate(passes):
            colsp = c1 - c0
            if pi == 1:
                bar = S.hard(pass0_last)
                for dc in range(DC):
                    def f(h, dc=dc):
                        return h.dma_start(out=A_w[:, dc * 1048:(dc + 1) * 1048],
                                           in_=wcat_h[dc * 128:(dc + 1) * 128, 2048:3096])
                    wload[1][dc] = S.dma("gpsimd", f, [bar] if dc == 0 else [])
            for tc in range(TC):
                buf = tc % 2
                g = [S.hard(xt_guard[buf])] if xt_guard[buf] is not None else []
                src = xT_h.rearrange("(dc p) l -> p dc l", p=128)[:, :, tc * 128:(tc + 1) * 128]

                def f(h, d=xt[buf], src=src):
                    return h.dma_start(out=d.rearrange("p (dc l) -> p dc l", l=128), in_=src)
                xt_load[buf] = S.dma("gpsimd", f, g)
                evacs = []
                for (a, b_) in cts:
                    w = b_ - a
                    bank = next_bank()
                    deps = pk.acq([bank]) + [xt_load[buf], wload[pi][DC - 1]]
                    tick = None
                    for dc in range(DC):
                        def fm(h, dc=dc, bank=bank, w=w, a=a, buf=buf, colsp=colsp, c0=c0):
                            return h.matmul(pk.sl(bank, 0, w),
                                            xt[buf][:, dc * 128:(dc + 1) * 128],
                                            A_w[:, dc * colsp + (a - c0):
                                                dc * colsp + (a - c0) + w],
                                            start=(dc == 0), stop=(dc == DC - 1))
                        tick = S.op("tensor", fm, deps if dc == 0 else [],
                                    inc="tk" if dc == DC - 1 else None)
                    mm_last = tick
                    if pi == 0:
                        pass0_last = tick
                    pk.wrote(bank, tick)
                    xt_guard[buf] = tick
                    if a < 3072:
                        def fe(h, bank=bank, w=w, tc=tc, a=a):
                            return h.activation(out=qkv3[:, tc, a:a + w],
                                                in_=pk.sl(bank, 0, w), func=AF.Silu)
                        te = S.op("scalar", fe, [tick], inc="ak")
                        if pi == 1:
                            v_evac_last = te
                    else:
                        def fe(h, bank=bank, w=w, tc=tc):
                            return h.tensor_copy(lr_all[:, tc * 24:(tc + 1) * 24],
                                                 pk.sl(bank, 0, w))
                        te = S.op("vector", fe, [tick], inc="vk")
                        lr_evac_last = te
                    pk.read(bank, te)
                    evacs.append(te)
                if pi == 0:
                    def fs(h, tc=tc):
                        return h.tensor_tensor(sqs[:, 0:2048], qkv3[:, tc, 0:2048],
                                               qkv3[:, tc, 0:2048], op=ALU.mult)
                    t1 = S.op("vector", fs, list(evacs), inc="vk")

                    def fr(h, tc=tc):
                        return h.tensor_reduce(
                            ssum_all[:, tc * 16:(tc + 1) * 16],
                            sqs.rearrange("p (g c) -> p g c", c=128),
                            axis=AX.X, op=ALU.add)
                    t_sqs_free = S.op("vector", fr, [t1], inc="vk")

        t_w012t = S.dma("gpsimd", lambda h: h.dma_start(
            out=wmat3("w012t", 0, NU * 384).rearrange("p (u c) -> p u c", c=384),
            in_=w012t_h.rearrange("(u p) c -> p u c", p=128)), [S.hard(pass0_last)])
        t_w012n = S.dma("gpsimd", lambda h: h.dma_start(
            out=wmat3("w012n", 0, NU * 384).rearrange("p (u c) -> p u c", c=384),
            in_=w012n_h.rearrange("(u p) c -> p u c", p=128)), [S.hard(pass0_last)])

        t_wn = None
        for u in range(NU):
            def f1(h, u=u):
                return h.tensor_tensor(wsq[:, 0:384], wmat3("w012n", u),
                                       wmat3("w012n", u), op=ALU.mult)
            a1t = S.op("vector", f1, [t_w012n], inc="vk")

            def f2(h, u=u):
                return h.tensor_reduce(wnorm0[:, 3 * u:3 * u + 3],
                                       wsq[:, 0:384].rearrange("p (g c) -> p g c", c=128),
                                       axis=AX.X, op=ALU.add)
            t_wn = S.op("vector", f2, [a1t], inc="vk")

        t_sq1 = S.op("scalar", lambda h: h.activation(
            out=inv_all[:, :], in_=ssum_all[:, :], func=AF.Sqrt),
            [t_sqs_free], inc="ak")
        t_sq2 = S.op("scalar", lambda h: h.activation(
            out=wnorm0[:, :], in_=wnorm0[:, :], func=AF.Sqrt), [t_wn], inc="ak")
        t_i1 = S.op("vector", lambda h: h.tensor_scalar(
            inv_all[:, :], inv_all[:, :], 1e-5, None, op0=ALU.add), [t_sq1], inc="vk")
        t_inv = S.op("vector", lambda h: h.reciprocal(inv_all[:, :], inv_all[:, :]),
                     [t_i1], inc="vk")
        t_inv = S.op("vector", lambda h: h.tensor_copy(tmpc[:, 0:1], inv_all[:, 0:1]),
                     [t_inv], inc="vk")
        t_lra = S.op("scalar", lambda h: h.activation(
            out=lr_all[:, :], in_=lr_all[:, :], func=AF.Exp,
            bias=f32(BASE_LR_INV)), [lr_evac_last], inc="ak")
        t_lr = S.op("scalar", lambda h: h.activation(
            out=lr_all[:, :], in_=lr_all[:, :], func=AF.Ln,
            bias=f32(1.0)), [t_lra], inc="ak")

        DBG_dump("inv", inv_all[:, :], [t_inv])
        DBG_dump("lr", lr_all[:, :], [S.hard(t_lr)])

        a123 = as_b(0, TC * 384)
        a123_3 = a123.rearrange("p (t c) -> p t c", c=384)
        sg_s = as_b(12288, 2048)
        sil_s = as_b(16384, 2048)
        rhs_cat = as_b(20480, TC * 256)
        rhs3 = rhs_cat.rearrange("p (t c) -> p t c", c=256)
        kfm = as_b(28672, 2048)
        vfm = as_b(32768, 2048)

        def a1s(tc):
            return a123_3[:, tc, 0:128]

        def a2s(tc):
            return a123_3[:, tc, 128:256]

        def a3s(tc):
            return a123_3[:, tc, 256:384]

        a1v = a123_3[:, :, 0:128]
        a2v = a123_3[:, :, 128:256]
        a3v = a123_3[:, :, 256:384]
        sgv = sg_s.rearrange("p (t c) -> p t c", c=128)
        silv = sil_s.rearrange("p (t c) -> p t c", c=128)

        def lrs(comp, tc, u):
            i = tc * 24 + comp * 8 + u
            return lr_all[:, i:i + 1]

        def invs(isq, tc, u):
            i = tc * 16 + (0 if isq else 8) + u
            return inv_all[:, i:i + 1]

        frob_last = None
        frob_first = True
        l4ready = []

        for u in range(NU):
            tsc = None
            for tc in range(TC):
                def fq(h, u=u, tc=tc):
                    return h.tensor_scalar(q_sl(u, tc), q_sl(u, tc),
                                           invs(True, tc, u), None, op0=ALU.mult)
                def fk(h, u=u, tc=tc):
                    return h.tensor_scalar(k_sl(u, tc), k_sl(u, tc),
                                           invs(False, tc, u), None, op0=ALU.mult)
                S.op("vector", fq, [t_inv] if tc == 0 else [], inc="vk")
                tsc = S.op("vector", fk, [], inc="vk")
            tsc = S.hard(tsc)

            def transpose16(src_fn, dst_fn, deps0):
                outs = []
                for half in range(2):
                    bank = 6 + half
                    bdeps = pk.acq([bank]) + deps0 + [t_ident]
                    tt = None
                    for i in range(8):
                        tc = half * 8 + i
                        def ft(h, bank=bank, i=i, tc=tc, src_fn=src_fn):
                            return h.transpose(pk.sl(bank, i * 128, 128, BF16),
                                               src_fn(tc), ident[:, :])
                        tt = S.op("tensor", ft, bdeps if i == 0 else [],
                                  inc="tk" if i == 7 else None)
                    pk.wrote(bank, tt)

                    def fe(h, bank=bank, half=half, dst_fn=dst_fn):
                        return h.tensor_copy(dst_fn(half), pk.sl(bank, 0, 1024, BF16))
                    te = S.op("vector", fe, [tt], inc="vk")
                    pk.read(bank, te)
                    outs.append(te)
                return outs

            tk_fm = transpose16(lambda tc, u=u: k_sl(u, tc),
                                lambda half: kfm[:, half * 1024:(half + 1) * 1024],
                                [tsc])
            tv_fm = transpose16(lambda tc, u=u: v_sl(u, tc),
                                lambda half: vfm[:, half * 1024:(half + 1) * 1024],
                                [S.hard(v_evac_last), S.hard(t_wn)])

            grp = []
            a_evac_last = None
            for tc in range(TC):
                bank = next_bank((3, 4, 5))
                deps = pk.acq([bank]) + tk_fm + tv_fm + [t_w012t, t_w012n]
                if u == 0 and tc == 0:
                    deps += [t_sqs_free, S.hard(t_lr)]

                def f1(h, u=u, tc=tc, bank=bank):
                    return h.matmul(pk.sl(bank, 0, 128), kfm[:, tc * 128:(tc + 1) * 128],
                                    wmat("w012t", 3 * u + 0), start=True, stop=True)

                def f2(h, u=u, tc=tc, bank=bank):
                    return h.matmul(pk.sl(bank, 128, 128), kfm[:, tc * 128:(tc + 1) * 128],
                                    wmat("w012t", 3 * u + 2), start=True, stop=True)

                def f3(h, u=u, tc=tc, bank=bank):
                    return h.matmul(pk.sl(bank, 256, 128), vfm[:, tc * 128:(tc + 1) * 128],
                                    wmat("w012n", 3 * u + 1), start=True, stop=True)
                S.op("tensor", f1, deps)
                S.op("tensor", f2, [])
                tm = S.op("tensor", f3, [], inc="tk")
                pk.wrote(bank, tm)
                grp.append((bank, tm, tc))
                if len(grp) == 3 or tc == TC - 1:
                    b0 = grp[0][0]
                    nb = len(grp)
                    contig = all(grp[i][0] == b0 + i for i in range(nb))
                    gdeps = [g[1] for g in grp]
                    t0 = grp[0][2]
                    if contig:
                        def fe(h, b0=b0, nb=nb, t0=t0):
                            return h.activation(out=a123_3[:, t0:t0 + nb, :],
                                                in_=pk.multi(b0, nb, 0, 384),
                                                func=AF.Identity)
                        a_evac_last = S.op("scalar", fe, gdeps, inc="ak")
                        for g_ in grp:
                            pk.read(g_[0], a_evac_last)
                    else:
                        for (gb, gt, gtc) in grp:
                            def fe(h, gb=gb, gtc=gtc):
                                return h.activation(out=a123_3[:, gtc, :],
                                                    in_=pk.sl(gb, 0, 384),
                                                    func=AF.Identity)
                            a_evac_last = S.op("scalar", fe, [gt], inc="ak")
                            pk.read(gb, a_evac_last)
                    grp = []

            if u == 0:
                DBG_dump("a123", a123[:, :], [a_evac_last])

            tsg = S.op("scalar", lambda h: h.activation(out=sg_s[:, :], in_=a123_3[:, :, 0:128],
                                                        func=AF.Sigmoid),
                       [a_evac_last], inc="ak")
            tt1 = S.op("vector", lambda h: h.tensor_tensor(silv, a1v, sgv, op=ALU.mult),
                       [tsg], inc="vk")
            tt2 = S.op("vector", lambda h: h.tensor_tensor(a1v, silv, sgv, op=ALU.mult),
                       [tt1], inc="vk")
            tt3 = S.op("vector", lambda h: h.tensor_tensor(a1v, sgv, a1v, op=ALU.subtract),
                       [tt2], inc="vk")
            tds = S.op("vector", lambda h: h.tensor_tensor(a1v, a1v, silv, op=ALU.add),
                       [tt3], inc="vk")
            tdg = S.op("vector", lambda h: h.tensor_tensor(sgv, a3v, a2v, op=ALU.mult),
                       [tds], inc="vk")
            t_r = tdg
            for tc in range(TC):
                def f(h, u=u, tc=tc):
                    return h.scalar_tensor_tensor(
                        rhs3[:, tc, 128:256], a3s(tc), lrs(2, tc, u),
                        silv[:, tc, :], op0=ALU.mult, op1=ALU.mult)
                t_r = S.op("vector", f, [S.hard(t_lr)] if (u == 0 and tc == 0) else [],
                           inc="vk")
            for tc in range(TC):
                def f(h, u=u, tc=tc):
                    return h.scalar_tensor_tensor(
                        a3s(tc), silv[:, tc, :], lrs(1, tc, u),
                        a2s(tc), op0=ALU.mult, op1=ALU.mult)
                t_r = S.op("vector", f, [], inc="vk")
            for tc in range(TC):
                def f(h, u=u, tc=tc):
                    return h.scalar_tensor_tensor(
                        rhs3[:, tc, 0:128], sgv[:, tc, :], lrs(0, tc, u),
                        a1v[:, tc, :], op0=ALU.mult, op1=ALU.mult)
                t_r = S.op("vector", f, [], inc="vk")
            t_r = S.hard(t_r)

            if u == 0:
                DBG_dump("rhs", rhs_cat[:, :], [t_r])
                DBG_dump("rhs1", a3v, [t_r])

            d0 = pk.acq([0]) + [t_r]
            d1 = pk.acq([1])
            tdw0 = tdw1 = None
            for tc in range(TC):
                def f0(h, u=u, tc=tc):
                    return h.matmul(pk.sl(0, 0, 256), k_sl(u, tc), rhs3[:, tc, 0:256],
                                    start=(tc == 0), stop=(tc == TC - 1))

                def fv(h, u=u, tc=tc):
                    return h.matmul(pk.sl(1, 0, 128), v_sl(u, tc), a3s(tc),
                                    start=(tc == 0), stop=(tc == TC - 1))
                tdw0 = S.op("tensor", f0, d0 if tc == 0 else [],
                            inc="tk" if tc == TC - 1 else None)
                tdw1 = S.op("tensor", fv, d1 if tc == 0 else [],
                            inc="tk" if tc == TC - 1 else None)
            pk.wrote(0, tdw0)
            pk.wrote(1, tdw1)

            tq_fm = transpose16(lambda tc, u=u: q_sl(u, tc),
                                lambda half, u=u: qfm_half(u, half),
                                [S.hard(tdw1), S.hard(tdw0)])

            def fe0(h, u=u):
                return h.activation(out=wmat("dwP", 3 * u + 0), in_=pk.sl(0, 0, 128),
                                    func=AF.Identity)

            def fe2(h, u=u):
                return h.activation(out=wmat("dwP", 3 * u + 2), in_=pk.sl(0, 128, 128),
                                    func=AF.Identity)

            def fe1(h, u=u):
                return h.activation(out=wmat("dwP", 3 * u + 1), in_=pk.sl(1, 0, 128),
                                    func=AF.Identity)
            te0 = S.op("scalar", fe0, [tdw0], inc="ak")
            te2 = S.op("scalar", fe2, [], inc="ak")
            te1 = S.op("scalar", fe1, [tdw1], inc="ak")
            pk.read(0, te0)
            pk.read(0, te2)
            pk.read(1, te1)

            bdeps = pk.acq([6]) + [te0, te2, te1, t_ident]
            ttr = None
            for m in range(3):
                def ft(h, u=u, m=m):
                    return h.transpose(pk.sl(6, m * 128, 128, BF16),
                                       wmat("dwP", 3 * u + m), ident[:, :])
                ttr = S.op("tensor", ft, bdeps if m == 0 else [],
                           inc="tk" if m == 2 else None)
            pk.wrote(6, ttr)

            def feo(h, u=u):
                return h.tensor_copy(wmat3("dwO", u), pk.sl(6, 0, 384, BF16))
            teo = S.op("vector", feo, [ttr], inc="vk")
            pk.read(6, teo)

            for m in range(3):
                def fsq(h, u=u, m=m):
                    return h.tensor_tensor(sil_s[:, 0:128], wmat("dwP", 3 * u + m),
                                           wmat("dwP", 3 * u + m), op=ALU.mult)
                tq1 = S.op("vector", fsq, [te0, te1, te2], inc="vk")

                mi = 3 * u + m

                def frd(h, mi=mi):
                    with nc.allow_low_precision(reason="bf16 matmul operand"):
                        return h.tensor_reduce(tmpcb24[:, mi:mi + 1], sil_s[:, 0:128],
                                               axis=AX.X, op=ALU.add)
                tq2 = S.op("vector", frd, [tq1], inc="vk")
                bdeps2 = (pk.acq([2]) if frob_first else []) + [tq2]
                frob_first = False

                def fmm(h, mi=mi):
                    return h.matmul(PS[0:1, 1024 + mi:1024 + mi + 1],
                                    tmpcb24[:, mi:mi + 1], ones_col[:, :],
                                    start=True, stop=True)
                frob_last = S.op("tensor", fmm, bdeps2, inc="tk")
            l4ready.append((te0, te1, te2, teo, tq_fm))
        pk.wrote(2, frob_last)

        DBG_dump("dwP", wmat("dwP", 0, 24 * 128), [l4ready[-1][0], l4ready[-1][1],
                                                   l4ready[-1][2]])
        DBG_dump("dwO", wmat("dwO", 0, 24 * 128), [l4ready[-1][3]])

        tfr = S.op("vector", lambda h: h.tensor_copy(frob_row[:, :], PS[0:1, 1024:1048]),
                   [frob_last], inc="vk")
        pk.read(2, tfr)
        tfs = S.op("scalar", lambda h: h.activation(out=frob_row[:, :], in_=frob_row[:, :],
                                                    func=AF.Sqrt), [tfr], inc="ak")
        tf1 = S.op("vector", lambda h: h.tensor_scalar(frob_row[:, :], frob_row[:, :],
                                                       1e-7, None, op0=ALU.add),
                   [tfs], inc="vk")
        tf2 = S.op("vector", lambda h: h.reciprocal(frob_row[:, :], frob_row[:, :]),
                   [tf1], inc="vk")
        tf2 = S.op("vector", lambda h: h.tensor_copy(tmpc[:, 0:1], ones_col[:, :]),
                   [tf2], inc="vk")
        tf2 = S.op("vector", lambda h: h.tensor_copy(tmpc[:, 1:2], ones_col[:, :]),
                   [tf2], inc="vk")
        bdeps = pk.acq([3]) + [tf2, S.hard(S.op("vector", lambda h: None) or None)] \
            if False else pk.acq([3]) + [tf2]
        tbc = None
        for m in range(24):
            def fmm(h, m=m):
                return h.matmul(PS[:, 1536 + m:1536 + m + 1], ones_row_f[:, :],
                                frob_row[0:1, m:m + 1], start=True, stop=True)
            tbc = S.op("tensor", fmm, bdeps if m == 0 else [],
                       inc="tk" if m == 23 else None)
        pk.wrote(3, tbc)
        tie = S.op("vector", lambda h: h.tensor_copy(invf_col[:, :], PS[:, 1536:1560]),
                   [tbc], inc="vk")
        pk.read(3, tie)
        tgap = S.op("vector", lambda h: h.tensor_copy(tmpcb[:, :], ones_col[:, :]),
                    [], inc="vk")
        tX0 = None
        for m in range(24):
            nat = "dwO" if m % 3 != 1 else "dwP"
            tor = "dwP" if m % 3 != 1 else "dwO"

            def fx(h, m=m, nat=nat):
                return h.tensor_scalar(wmat("nsX", m), wmat(nat, m),
                                       invf_col[:, m:m + 1], None, op0=ALU.mult)

            def fy(h, m=m, tor=tor):
                return h.tensor_scalar(wmat("nsY", m), wmat(tor, m),
                                       invf_col[:, m:m + 1], None, op0=ALU.mult)
            S.op("vector", fx, [tie, tgap] if m == 0 else [], inc="vk")
            tX0 = S.op("vector", fy, [], inc="vk")
        tX0 = S.hard(tX0)

        ns_v_last = [tX0] * NU
        l3_mm_last = None
        for it in range(5):
            a_c, b_c, c_c = NS_COEFFS[it]
            for u in range(NU):
                bA = next_bank((0, 1, 2, 3, 4, 5))
                bB = next_bank((0, 1, 2, 3, 4, 5))
                bC = next_bank((0, 1, 2, 3, 4, 5))
                bD = next_bank((0, 1, 2, 3, 4, 5))
                dA = pk.acq([bA]) + [ns_v_last[u]]
                tA = None
                for m in range(3):
                    def fA(h, u=u, m=m, bA=bA):
                        return h.matmul(pk.sl(bA, m * 128, 128), wmat("nsY", 3 * u + m),
                                        wmat("nsY", 3 * u + m), start=True, stop=True)
                    tA = S.op("tensor", fA, dA if m == 0 else [],
                              inc="tk" if m == 2 else None)
                pk.wrote(bA, tA)

                def fAe(h, u=u, bA=bA):
                    return h.tensor_copy(wmat3("nsA", u), pk.sl(bA, 0, 384, BF16)
                                         if False else pk.sl(bA, 0, 384))
                def fAe2(h, u=u, bA=bA):
                    return h.tensor_copy(wmat3("nsA", u), pk.sl(bA, 0, 384))
                tAe = S.op("vector", fAe2, [tA], inc="vk")
                pk.read(bA, tAe)

                dB = pk.acq([bB]) + [tAe]
                tA2 = None
                for m in range(3):
                    def fA2(h, u=u, m=m, bB=bB):
                        return h.matmul(pk.sl(bB, m * 128, 128), wmat("nsA", 3 * u + m),
                                        wmat("nsA", 3 * u + m), start=True, stop=True)
                    tA2 = S.op("tensor", fA2, dB if m == 0 else [],
                               inc="tk" if m == 2 else None)
                pk.wrote(bB, tA2)
                def fB1(h, u=u, bB=bB, c_c=c_c):
                    return h.tensor_scalar(wmat3("nsB", u), pk.sl(bB, 0, 384),
                                           float(c_c), None, op0=ALU.mult)
                tB1 = S.op("vector", fB1, [tA2], inc="vk")
                pk.read(bB, tB1)

                def fB2(h, u=u, b_c=b_c):
                    return h.scalar_tensor_tensor(wmat3("nsB", u), wmat3("nsA", u),
                                                  float(b_c), wmat3("nsB", u),
                                                  op0=ALU.mult, op1=ALU.add)
                tB2 = S.op("vector", fB2, [tB1], inc="vk")

                dC = pk.acq([bC]) + [tB2]
                tBX = None
                for m in range(3):
                    def fBX(h, u=u, m=m, bC=bC):
                        return h.matmul(pk.sl(bC, m * 128, 128), wmat("nsB", 3 * u + m),
                                        wmat("nsX", 3 * u + m), start=True, stop=True)
                    tBX = S.op("tensor", fBX, dC if m == 0 else [],
                               inc="tk" if m == 2 else None)
                pk.wrote(bC, tBX)
                dD = pk.acq([bD])
                tXTB = None
                for m in range(3):
                    def fXTB(h, u=u, m=m, bD=bD):
                        return h.matmul(pk.sl(bD, m * 128, 128), wmat("nsX", 3 * u + m),
                                        wmat("nsB", 3 * u + m), start=True, stop=True)
                    tXTB = S.op("tensor", fXTB, dD if m == 0 else [],
                                inc="tk" if m == 2 else None)
                pk.wrote(bD, tXTB)
                l3_mm_last = tXTB

                def fXn(h, u=u, bC=bC, a_c=a_c):
                    return h.scalar_tensor_tensor(wmat3("nsX", u), wmat3("nsX", u),
                                                  float(a_c), pk.sl(bC, 0, 384),
                                                  op0=ALU.mult, op1=ALU.add)
                tXn = S.op("vector", fXn, [tBX, tXTB], inc="vk")
                pk.read(bC, tXn)

                def fYn(h, u=u, bD=bD, a_c=a_c):
                    return h.scalar_tensor_tensor(wmat3("nsY", u), wmat3("nsY", u),
                                                  float(a_c), pk.sl(bD, 0, 384),
                                                  op0=ALU.mult, op1=ALU.add)
                tYn = S.op("vector", fYn, [tXn], inc="vk")
                pk.read(bD, tYn)
                ns_v_last[u] = tYn

        DBG_dump("nsX", wmat("nsX", 0, 24 * 128), [ns_v_last[-1]])

        def wfin_m(m, n=128):
            return as_b(28672 + m * 256, n)

        wnsc = as_b(12288, 2048)
        l4_v_last = None
        for u in range(NU):
            for m3 in range(3):
                m = 3 * u + m3
                tor = "dwP" if m3 != 1 else "dwO"
                nat = "dwO" if m3 != 1 else "dwP"

                def fT1(h, u=u, m3=m3, m=m):
                    return h.tensor_tensor(wfin_m(m), wmat("w012t", m),
                                           wmat("nsY", m), op=ALU.add)
                tt1 = S.op("vector", fT1, [ns_v_last[u]], inc="vk")

                def fT2(h, m=m, tor=tor):
                    return h.tensor_tensor(wfin_m(m), wfin_m(m), wmat(tor, m),
                                           op=ALU.add)
                S.op("vector", fT2, [tt1], inc="vk")

                def fN1(h, m=m, m3=m3):
                    return h.tensor_tensor(wnsc[:, m3 * 128:(m3 + 1) * 128],
                                           wmat("w012n", m), wmat("nsX", m), op=ALU.add)
                tn1 = S.op("vector", fN1, [], inc="vk")

                def fN2(h, m=m, m3=m3, nat=nat):
                    return h.tensor_tensor(wnsc[:, m3 * 128:(m3 + 1) * 128],
                                           wnsc[:, m3 * 128:(m3 + 1) * 128],
                                           wmat(nat, m), op=ALU.add)
                S.op("vector", fN2, [tn1], inc="vk")
            def fsq(h, u=u):
                return h.tensor_tensor(wnsc[:, 384:768], wnsc[:, 0:384],
                                       wnsc[:, 0:384], op=ALU.mult)
            tq = S.op("vector", fsq, [], inc="vk")

            def frd(h, u=u):
                return h.tensor_reduce(rn_all[:, 3 * u:3 * u + 3],
                                       wnsc[:, 384:768].rearrange("p (g c) -> p g c", c=128),
                                       axis=AX.X, op=ALU.add)
            l4_v_last = S.op("vector", frd, [tq], inc="vk")

        DBG_dump("rn_raw", rn_all[:, :], [l4_v_last])
        DBG_dump("wn0", wnorm0[:, :], [S.hard(t_sq2)])
        trs = S.op("scalar", lambda h: h.activation(out=rn_all[:, :], in_=rn_all[:, :],
                                                    func=AF.Sqrt), [l4_v_last], inc="ak")
        tr1 = S.op("vector", lambda h: h.tensor_scalar(rn_all[:, :], rn_all[:, :],
                                                       1e-5, None, op0=ALU.add),
                   [trs], inc="vk")
        tr2 = S.op("vector", lambda h: h.reciprocal(rn_all[:, :], rn_all[:, :]),
                   [tr1], inc="vk")
        tr2 = S.op("vector", lambda h: h.tensor_copy(tmpc[:, 0:1], ones_col[:, :]),
                   [tr2], inc="vk")
        tr2 = S.op("vector", lambda h: h.tensor_copy(tmpc[:, 1:2], ones_col[:, :]),
                   [tr2], inc="vk")
        ts_all = S.op("vector", lambda h: h.tensor_tensor(s_all[:, :], rn_all[:, :],
                                                          wnorm0[:, :], op=ALU.mult),
                      [tr2, t_sq2], inc="vk")
        s_mid = s_all.rearrange("p (m c) -> p m c", c=3)[:, :, 1]
        tsg_ = S.op("vector", lambda h: h.tensor_copy(tmpc[:, 0:1], ones_col[:, :]),
                    [ts_all], inc="vk")
        tsg_ = S.op("vector", lambda h: h.tensor_copy(tmpc[:, 1:2], ones_col[:, :]),
                    [tsg_], inc="vk")
        ts1c = S.op("vector", lambda h: h.tensor_scalar(s1c[:, :], s_mid, osc_s[:, :],
                                                        None, op0=ALU.mult),
                    [tsg_, t_osc], inc="vk")
        l4_done = S.hard(ts1c)

        DBG_dump("wfin", wfin_m(0, 24 * 128), [l4_done])
        DBG_dump("s_all", s_all[:, :], [l4_done])

        wo_loads = []
        wo_dep = [l4_done, S.hard(l3_mm_last)]
        for fc in range(16):
            def f(h, fc=fc):
                return h.dma_start(out=A_w[:, fc * 2048:(fc + 1) * 2048],
                                   in_=wo_h[fc * 128:(fc + 1) * 128, :])
            wo_loads.append(S.dma("gpsimd", f, wo_dep if fc == 0 else []))

        g_s = as_b(0, 2048)
        h_s = as_b(4096, 2048)
        sg2_s = as_b(8192, 2048)
        sq2_s = as_b(12288, 2048)
        srow = as_b(16384, 2048, F32)[0:1, :]

        l5_mm_last = None
        srow_dmas = []
        ofm_ticks = [None] * NU
        for u in range(NU):
            deps = pk.acq([0, 1, 2, 3]) + [l4_done]
            tg = None
            for j in range(4):
                def fg(h, u=u, j=j):
                    return h.matmul(pk.sl(j, 0, 512), wfin_m(3 * u + 0),
                                    qfm_mov(u, 4 * j, 4), start=True, stop=True)
                tg = S.op("tensor", fg, deps if j == 0 else [],
                          inc="tk" if j == 3 else None)
            for j in range(4):
                pk.wrote(j, tg)
            tsg2 = S.op("scalar", lambda h, u=u: h.activation(
                out=sg2_s[:, :], in_=pk.multi(0, 4), func=AF.Sigmoid,
                scale=s_all[:, 3 * u:3 * u + 1]), [tg, S.hard(ts_all)], inc="ak")
            tgid = S.op("scalar", lambda h, u=u: h.activation(
                out=g_s[:, :], in_=pk.multi(0, 4), func=AF.Identity,
                scale=s_all[:, 3 * u:3 * u + 1]), [], inc="ak")
            for j in range(4):
                pk.read(j, tsg2)
                pk.read(j, tgid)
            deps = pk.acq([4, 5, 6, 7])
            th = None
            for j in range(4):
                def fh(h, u=u, j=j):
                    return h.matmul(pk.sl(4 + j, 0, 512), wfin_m(3 * u + 2),
                                    qfm_mov(u, 4 * j, 4), start=True, stop=True)
                th = S.op("tensor", fh, deps if j == 0 else [],
                          inc="tk" if j == 3 else None)
            for j in range(4):
                pk.wrote(4 + j, th)
            thid = S.op("scalar", lambda h, u=u: h.activation(
                out=h_s[:, :], in_=pk.multi(4, 4), func=AF.Identity,
                scale=s_all[:, 3 * u + 2:3 * u + 3]), [th], inc="ak")
            for j in range(4):
                pk.read(4 + j, thid)
            tgt = S.op("vector", lambda h: h.tensor_tensor(g_s[:, :], g_s[:, :],
                                                           sg2_s[:, :], op=ALU.mult),
                       [tsg2, tgid], inc="vk")
            tgh = S.op("vector", lambda h: h.tensor_tensor(h_s[:, :], g_s[:, :],
                                                           h_s[:, :], op=ALU.mult),
                       [tgt, thid], inc="vk")
            deps = pk.acq([0, 1, 2, 3]) + [tgh]
            to = None
            for j in range(4):
                def fo(h, u=u, j=j):
                    return h.matmul(pk.sl(j, 0, 512), wfin_m(3 * u + 1),
                                    h_s[:, j * 512:(j + 1) * 512], start=True, stop=True)
                to = S.op("tensor", fo, deps if j == 0 else [],
                          inc="tk" if j == 3 else None)
            for j in range(4):
                pk.wrote(j, to)
            l5_mm_last = to
            tofm = S.op("scalar", lambda h, u=u: h.activation(
                out=ofm_v(u), in_=pk.multi(0, 4), func=AF.Identity,
                scale=s1c[:, u:u + 1]), [to, S.hard(ts1c)], inc="ak")
            tsq = S.op("scalar", lambda h, u=u: h.activation(
                out=sq2_s[:, :], in_=pk.multi(0, 4), func=AF.Square,
                scale=s_all[:, 3 * u + 1:3 * u + 2]), [], inc="ak")
            for j in range(4):
                pk.read(j, tofm)
                pk.read(j, tsq)
            ofm_ticks[u] = tofm
            deps = pk.acq([4, 5, 6, 7]) + [tsq]
            tno = None
            for j in range(4):
                def fn(h, j=j):
                    return h.matmul(PS[0:1, (4 + j) * 512:(4 + j) * 512 + 512],
                                    ones_col[:, :], sq2_s[:, j * 512:(j + 1) * 512],
                                    start=True, stop=True)
                tno = S.op("tensor", fn, deps if j == 0 else [],
                           inc="tk" if j == 3 else None)
            for j in range(4):
                pk.wrote(4 + j, tno)
            sdep = [tno] + ([srow_dmas[-1]] if srow_dmas else [])
            tsr = S.op("scalar", lambda h: h.activation(
                out=srow.rearrange("o (b c) -> o b c", c=512),
                in_=pk.multi(4, 4)[0:1], func=AF.Identity), sdep, inc="ak")
            for j in range(4):
                pk.read(4 + j, tsr)

            def fdma(h, u=u):
                return h.dma_start(out=nrow_d[u:u + 1, :], in_=srow)
            tsd = S.dma("sync", fdma, [tsr])
            srow_dmas.append(tsd)

        nrm8 = as_b(28672, 2048, F32)[0:8, :]
        nrm8b = as_b(24576, 2048)[0:8, :]
        rrow = as_b(16384, 2048)[0:1, :]
        rb = as_b(20480, 2048)

        tn8 = S.dma("sync", lambda h: h.dma_start(out=nrm8, in_=nrow_d[:, :]),
                    srow_dmas + [S.hard(l5_mm_last)])
        tln = S.op("scalar", lambda h: h.activation(out=nrm8, in_=nrm8, func=AF.Ln,
                                                    scale=f32(1.0 / 128.0),
                                                    bias=f32(1e-5)), [tn8], inc="ak")
        tex = S.op("scalar", lambda h: h.activation(out=nrm8b, in_=nrm8, func=AF.Exp,
                                                    scale=f32(-0.5)), [tln], inc="ak")
        trd = S.dma("sync", lambda h: h.dma_start(out=rrow_d[:, :], in_=nrm8b), [tex])

        ag_sends = []
        prev_tkm = None
        for u in range(NU):
            rdeps = [trd] if u == 0 else [S.hard(prev_tkm)]
            trl = S.dma("sync", lambda h, u=u: h.dma_start(out=rrow, in_=rrow_d[u:u + 1, :]),
                        rdeps)
            deps = pk.acq([0, 1, 2, 3]) + [trl]
            tkm = None
            for j in range(4):
                def fk(h, j=j):
                    return h.matmul(pk.sl(j, 0, 512), ones_row[:, :],
                                    rrow[0:1, j * 512:(j + 1) * 512],
                                    start=True, stop=True)
                tkm = S.op("tensor", fk, deps if j == 0 else [],
                           inc="tk" if j == 3 else None)
            for j in range(4):
                pk.wrote(j, tkm)
            prev_tkm = tkm
            trb = S.op("vector", lambda h: h.tensor_copy(rb[:, :], pk.multi(0, 4)),
                       [tkm], inc="vk")
            for j in range(4):
                pk.read(j, trb)

            def fsc(h, u=u):
                return h.tensor_tensor(ofm_v(u), ofm_v(u),
                                       rb.rearrange("p (t c) -> p t c", c=128),
                                       op=ALU.mult)
            tsc2 = S.op("vector", fsc, [trb, ofm_ticks[u]], inc="vk")

            def fsend(h, u=u):
                return h.dma_start(out=ag_in[u * 128:(u + 1) * 128, :],
                                   in_=qkv3[:, 8:16, u * 128:(u + 1) * 128])
            ag_sends.append(S.dma("sync", fsend, [S.hard(tsc2)]))
        ofm_scaled_last = tsc2

        tag = S.op("gpsimd", lambda h: h.collective_compute(
            "AllGather", ALU.bypass,
            replica_groups=[[0, 1], [2, 3], [4, 5], [6, 7]],
            ins=[ag_in.ap().opt()], outs=[ag_out.ap().opt()]),
            ag_sends, inc="cc")
        tag = S.hard(tag)

        tc0 = tc1 = None
        for j in range(NU):
            def fc0(h, j=j):
                return h.dma_start(
                    out=qkv3[:, 0:8, 2048 + j * 128:2048 + (j + 1) * 128],
                    in_=ag_out[j * 128:(j + 1) * 128, :].rearrange(
                        "p (l t) -> p l t", t=128))
            tc0 = S.dma("sync", fc0, [tag, S.hard(l5_mm_last)] if j == 0 else [])

            def fc1(h, j=j):
                return h.dma_start(
                    out=qkv3[:, 0:8, 1024 + j * 128:1024 + (j + 1) * 128],
                    in_=ag_out[1024 + j * 128:1024 + (j + 1) * 128, :].rearrange(
                        "p (l t) -> p l t", t=128))
            tc1 = S.dma("sync", fc1, [tag] if j == 0 else [])
        tp1 = S.op("vector", lambda h: h.tensor_tensor(opart_3d, opart_3d, c1_3d,
                                                       op=ALU.add), [tc0, tc1], inc="vk")
        tp2 = S.op("vector", lambda h: h.tensor_tensor(opart_3d, opart_3d, mine_3d,
                                                       op=ALU.subtract),
                   [tp1, S.hard(ofm_scaled_last)], inc="vk")
        t_opart = S.hard(tp2)

        DBG_dump("ofm", qkv3[:, :, 0:1024], [S.hard(ofm_scaled_last)])
        DBG_dump("opart", qkv3[:, 0:8, 2048:3072], [t_opart])

        if BARRIERS:
            S.barrier()
        xres_b = [as_b(0, 2048, F32), as_b(8192, 2048, F32)]
        r1_b = [as_b(16384, 2048, F32), as_b(24576, 2048, F32)]
        sqw = as_b(32768, 2048)
        xres_t = [None, None]
        xres_guard = [None, None]
        ln_q = []
        x2_ticks = [None] * LC
        wo_mm_last = None
        xm_last = None

        for lc in range(LC):
            buf = lc % 2
            if lc >= 1:
                x2_ticks[lc - 1], xm_last = ln_q[lc - 1]()
            gdeps = [xres_guard[buf]] if xres_guard[buf] is not None else []

            def fx(h, lc=lc, buf=buf):
                return h.dma_start(out=xres_b[buf].rearrange("p (a c) -> p a c", c=512)
                                   if False else xres_b[buf][:, :],
                                   in_=xres_h[lc * 128:(lc + 1) * 128, :])
            xres_t[buf] = S.dma("sync", fx, gdeps)
            bs = [0, 1, 2, 3] if lc % 2 == 0 else [4, 5, 6, 7]
            deps = pk.acq(bs) + [t_opart, S.hard(ofm_scaled_last), wo_loads[-1]]
            tmm = None
            for ct in range(4):
                for fc in range(16):
                    if fc < 8:
                        stat = ofm_sl(fc, lc)
                    else:
                        stat = opart_sl(fc - 8, lc)

                    def fmo(h, stat=stat, ct=ct, fc=fc, bs=bs):
                        return h.matmul(pk.sl(bs[ct], 0, 512), stat,
                                        A_w[:, fc * 2048 + ct * 512:
                                            fc * 2048 + ct * 512 + 512],
                                        start=(fc == 0), stop=(fc == 15))
                    tmm = S.op("tensor", fmo, deps if (ct == 0 and fc == 0) else [],
                               inc="tk" if (ct == 3 and fc == 15) else None)
            for b_ in bs:
                pk.wrote(b_, tmm)
            wo_mm_last = tmm
            def fr1(h, bs0=bs[0], buf=buf):
                return h.tensor_tensor(r1_b[buf].rearrange("p (a c) -> p a c", c=512),
                                       pk.multi(bs0, 4),
                                       xres_b[buf].rearrange("p (a c) -> p a c", c=512),
                                       op=ALU.add)
            r1war = [S.hard(x2_ticks[lc - 2])] if lc >= 2 else []
            tr1_ = S.op("vector", fr1, [tmm, xres_t[buf]] + r1war, inc="vk")
            if lc == 6:
                DBG_dump("r1c6", r1_b[0][:, :], [tr1_])
            for b_ in bs:
                pk.read(b_, tr1_)
            xres_guard[buf] = S.hard(tr1_)
            sl_ = lc % 2
            tsum = S.op("vector", lambda h, buf=buf, sl_=sl_: h.tensor_scalar(
                r1_b[buf][:, :], r1_b[buf][:, :], 1.0, None, op0=ALU.mult,
                op1=ALU.add, accum_out=mv[:, 2 * sl_:2 * sl_ + 1]),
                [tr1_], inc="vk")
            tsq_ = S.op("vector", lambda h, buf=buf, sl_=sl_: h.scalar_tensor_tensor(
                sqw[:, :], r1_b[buf][:, :], 1.0, r1_b[buf][:, :],
                op0=ALU.mult, op1=ALU.mult,
                accum_out=mv[:, 2 * sl_ + 1:2 * sl_ + 2]), [tsum], inc="vk")
            tm_ = S.op("scalar", lambda h, sl_=sl_: h.activation(
                out=mv[:, 2 * sl_:2 * sl_ + 1], in_=mv[:, 2 * sl_:2 * sl_ + 1],
                func=AF.Identity, scale=f32(1.0 / 2048.0)), [tsq_], inc="ak")
            tms = S.op("scalar", lambda h, sl_=sl_: h.activation(
                out=msq[:, sl_:sl_ + 1], in_=mv[:, 2 * sl_:2 * sl_ + 1],
                func=AF.Square), [tm_], inc="ak")
            tvv = S.op("scalar", lambda h, sl_=sl_: h.activation(
                out=mv[:, 2 * sl_ + 1:2 * sl_ + 2],
                in_=mv[:, 2 * sl_ + 1:2 * sl_ + 2],
                func=AF.Identity, scale=f32(1.0 / 2048.0)), [tms], inc="ak")
            tag_ = S.op("vector", lambda h, sl_=sl_: h.tensor_tensor(
                mv[:, 2 * sl_ + 1:2 * sl_ + 2], mv[:, 2 * sl_ + 1:2 * sl_ + 2],
                msq[:, sl_:sl_ + 1], op=ALU.subtract), [S.hard(tvv)], inc="vk")
            tl1 = S.op("scalar", lambda h, sl_=sl_: h.activation(
                out=isig[:, sl_:sl_ + 1], in_=mv[:, 2 * sl_ + 1:2 * sl_ + 2],
                func=AF.Ln, bias=f32(1e-5)), [tag_], inc="ak")
            tl2 = S.op("scalar", lambda h, sl_=sl_: h.activation(
                out=isig[:, sl_:sl_ + 1], in_=isig[:, sl_:sl_ + 1], func=AF.Exp,
                scale=f32(-0.5)), [tl1], inc="ak")
            tnbA = S.op("scalar", lambda h, sl_=sl_: h.activation(
                out=nbias[:, sl_:sl_ + 1], in_=mv[:, 2 * sl_:2 * sl_ + 1],
                func=AF.Identity, scale=isig[:, sl_:sl_ + 1]), [tl2], inc="ak")
            tnb2 = S.op("scalar", lambda h, sl_=sl_: h.activation(
                out=nbias[:, sl_:sl_ + 1], in_=nbias[:, sl_:sl_ + 1],
                func=AF.Identity, scale=np.float32(-1.0)), [tnbA], inc="ak")

            if lc == 6 and "lnv6" in DEBUG:
                def fdump(h, sl_=sl_):
                    return h.tensor_copy(dbg_lnv_stage[:, 0:6],
                                         mv[:, 0:2])
                tcp1 = S.op("vector", lambda h, sl_=sl_: h.tensor_copy(
                    dbg_lnv_stage[:, 0:2], mv[:, 2 * sl_:2 * sl_ + 2]), [tnb2], inc="vk")
                tcp2 = S.op("vector", lambda h, sl_=sl_: h.tensor_copy(
                    dbg_lnv_stage[:, 2:3], isig[:, sl_:sl_ + 1]), [tcp1], inc="vk")
                tcp3 = S.op("vector", lambda h, sl_=sl_: h.tensor_copy(
                    dbg_lnv_stage[:, 3:4], nbias[:, sl_:sl_ + 1]), [tcp2], inc="vk")
                DBG_dump("lnv6", dbg_lnv_stage[:, :], [tcp3])

            def emit_apply(lc=lc, sl_=sl_, buf=buf, tr1t=tr1_, tnbt=tnb2):
                S.op("scalar", lambda h: h.activation(
                    out=tmpc[:, 1:2], in_=tmpc[:, 1:2], func=AF.Identity), [],
                    inc="ak")

                def fap(h):
                    return h.activation(out=x2_sl(lc), in_=r1_b[buf].rearrange(
                        "p (a c) -> p a c", c=1024), func=AF.Identity,
                        scale=isig[:, sl_:sl_ + 1], bias=nbias[:, sl_:sl_ + 1])
                dep_extra = [t_opart] if lc < 4 else []
                ta = S.op("scalar", fap, [tr1t, tnbt] + dep_extra, inc="ak")
                def fxm(h, lc=lc):
                    return h.tensor_reduce(xmax8[:, lc:lc + 1], x2_sl(lc),
                                           axis=AX.XY, op=ALU.max,
                                           apply_absolute_value=True)
                tm2 = S.op("vector", fxm, [ta], inc="vk")
                return ta, tm2

            ln_q.append(emit_apply)
        S.op("scalar", lambda h: h.activation(out=tmpc[:, 0:1], in_=tmpc[:, 0:1],
                                              func=AF.Identity), [], inc="ak")
        x2_ticks[LC - 1], xm_last = ln_q[LC - 1]()
        x2_done = S.hard(x2_ticks[LC - 1])

        DBG_dump("x2", qkv3[:, :, 1024:2048], [x2_done])

        txg = S.op("vector", lambda h: h.tensor_copy(tmpc[:, 0:1], ones_col[:, :]),
                   [xm_last], inc="vk")
        txg = S.op("vector", lambda h: h.tensor_copy(tmpc[:, 1:2], ones_col[:, :]),
                   [txg], inc="vk")
        txr = S.op("vector", lambda h: h.tensor_reduce(
            xmax[:, :], xmax8[:, :], axis=AX.X, op=ALU.max), [txg], inc="vk")
        txr = S.op("vector", lambda h: h.tensor_copy(tmpc[:, 0:1], ones_col[:, :]),
                   [txr], inc="vk")
        txr = S.op("vector", lambda h: h.tensor_copy(tmpc[:, 1:2], ones_col[:, :]),
                   [txr], inc="vk")
        DBG_dump("xmloc", xmax[:, 0:1], [S.hard(txr)])
        tb1 = S.dma("sync", lambda h: h.dma_start(out=col_d.rearrange('(p o) -> p o', o=1), in_=xmax[:, 0:1]),
                    [S.hard(txr)])
        tb2 = S.dma("sync", lambda h: h.dma_start(
            out=row128[:, :], in_=col_d.rearrange("(o p) -> o p", o=1)), [tb1])
        tb3 = S.op("vector", lambda h: h.tensor_reduce(sc11[0:1, 2:3], row128[:, :],
                                                       axis=AX.X, op=ALU.max),
                   [tb2], inc="vk")
        tb3 = S.op("vector", lambda h: h.tensor_copy(tmpc[:, 0:1], ones_col[:, :]),
                   [tb3], inc="vk")
        tb3 = S.op("vector", lambda h: h.tensor_copy(tmpc[:, 1:2], ones_col[:, :]),
                   [tb3], inc="vk")
        tb4 = S.op("vector", lambda h: h.tensor_scalar(sc11[0:1, 2:3], sc11[0:1, 2:3],
                                                       1e-6, None, op0=ALU.max),
                   [tb3], inc="vk")
        tb4 = S.op("vector", lambda h: h.tensor_copy(tmpc[:, 0:1], ones_col[:, :]),
                   [tb4], inc="vk")
        tb4 = S.op("vector", lambda h: h.tensor_copy(tmpc[:, 1:2], ones_col[:, :]),
                   [tb4], inc="vk")
        tb5 = S.dma("sync", lambda h: h.dma_start(out=ar_in[:, :], in_=sc11[0:1, 2:3]),
                    [S.hard(tb4)])
        tar = S.op("gpsimd", lambda h: h.collective_compute(
            "AllReduce", ALU.max, replica_groups=[list(range(8))],
            ins=[ar_in.ap().opt()], outs=[ar_out.ap().opt()]), [tb5], inc="cc")
        tar = S.hard(tar)
        txs = S.dma("sync", lambda h: h.dma_start(out=sc11[0:1, 2:3], in_=ar_out[:, :]),
                    [tar])

        if BARRIERS:
            S.barrier()
        fw_loads = []
        ws_ticks = []
        for kc in range(16):
            def f(h, kc=kc):
                return h.dma_start(out=fw_ch(kc), in_=fwT_h[kc * 128:(kc + 1) * 128, :])
            tl = S.dma("gpsimd", f, [S.hard(wo_mm_last)] if kc == 0 else [])
            fw_loads.append(tl)

            def fwm(h, kc=kc):
                return h.tensor_reduce(wmax16[:, kc:kc + 1], fw_ch(kc), axis=AX.X,
                                       op=ALU.max, apply_absolute_value=True)
            ws_ticks.append(S.op("vector", fwm, [tl, x2_done], inc="vk"))
        twg = S.op("vector", lambda h: h.tensor_copy(tmpc[:, 0:1], ones_col[:, :]),
                   ws_ticks, inc="vk")
        twg = S.op("vector", lambda h: h.tensor_copy(tmpc[:, 1:2], ones_col[:, :]),
                   [twg], inc="vk")
        twr = S.op("vector", lambda h: h.tensor_reduce(
            wmax[:, :], wmax16[:, :], axis=AX.X, op=ALU.max),
            [twg], inc="vk")
        twr = S.op("vector", lambda h: h.tensor_copy(tmpc[:, 0:1], ones_col[:, :]),
                   [twr], inc="vk")
        twr = S.op("vector", lambda h: h.tensor_copy(tmpc[:, 1:2], ones_col[:, :]),
                   [twr], inc="vk")
        tw1 = S.dma("sync", lambda h: h.dma_start(out=col_d2.rearrange('(p o) -> p o', o=1), in_=wmax[:, 0:1]),
                    [S.hard(twr)])
        tw2 = S.dma("sync", lambda h: h.dma_start(
            out=row128b[:, :], in_=col_d2.rearrange("(o p) -> o p", o=1)), [tw1])
        tw3 = S.op("vector", lambda h: h.tensor_reduce(sc11[0:1, 0:1], row128b[:, :],
                                                       axis=AX.X, op=ALU.max),
                   [tw2], inc="vk")
        tw3 = S.op("vector", lambda h: h.tensor_copy(tmpc[:, 0:1], ones_col[:, :]),
                   [tw3], inc="vk")
        tw3 = S.op("vector", lambda h: h.tensor_copy(tmpc[:, 1:2], ones_col[:, :]),
                   [tw3], inc="vk")
        tw4 = S.op("vector", lambda h: h.tensor_scalar(sc11[0:1, 0:1], sc11[0:1, 0:1],
                                                       1e-6, None, op0=ALU.max),
                   [tw3], inc="vk")
        tw4 = S.op("vector", lambda h: h.tensor_copy(tmpc[:, 0:1], ones_col[:, :]),
                   [tw4], inc="vk")
        tw5 = S.op("vector", lambda h: h.reciprocal(sc11[0:1, 1:2], sc11[0:1, 0:1]),
                   [tw4], inc="vk")
        tw5 = S.op("vector", lambda h: h.tensor_copy(tmpc[:, 0:1], ones_col[:, :]),
                   [tw5], inc="vk")
        tx1 = S.op("vector", lambda h: h.reciprocal(sc11[0:1, 4:5], sc11[0:1, 2:3]),
                   [txs], inc="vk")
        tx1 = S.op("vector", lambda h: h.tensor_copy(tmpc[:, 1:2], ones_col[:, :]),
                   [tx1], inc="vk")
        tx2 = S.op("vector", lambda h: h.tensor_tensor(sc11[0:1, 3:4], sc11[0:1, 2:3],
                                                       sc11[0:1, 0:1], op=ALU.mult),
                   [tx1, tw5], inc="vk")
        tx2 = S.op("vector", lambda h: h.tensor_copy(tmpc[:, 0:1], ones_col[:, :]),
                   [tx2], inc="vk")
        tx2 = S.op("vector", lambda h: h.tensor_copy(tmpc[:, 1:2], ones_col[:, :]),
                   [tx2], inc="vk")
        bdeps = pk.acq([4]) + [S.hard(tw5), S.hard(tx2)]
        tbw = None
        for i, cell in enumerate((1, 4, 0, 2)):
            def fbc(h, cell=cell, i=i):
                return h.matmul(PS[:, 4 * 512 + i:4 * 512 + i + 1], ones_row_f[:, :],
                                sc11[0:1, cell:cell + 1], start=True, stop=True)
            tbw = S.op("tensor", fbc, bdeps if i == 0 else [],
                       inc="tk" if i == 3 else None)
        pk.wrote(4, tbw)
        tbe = S.op("vector", lambda h: h.tensor_copy(wsinv_c[:, :], PS[:, 2048:2049]),
                   [tbw], inc="vk")
        tbe2 = S.op("vector", lambda h: h.tensor_copy(xsinv_c[:, :], PS[:, 2049:2050]),
                    [], inc="vk")
        tbe3 = S.op("vector", lambda h: h.tensor_copy(ws_c[:, :], PS[:, 2050:2051]),
                    [], inc="vk")
        tbe4 = S.op("vector", lambda h: h.tensor_copy(xs_c[:, :], PS[:, 2051:2052]),
                    [], inc="vk")
        pk.read(4, tbe)
        pk.read(4, tbe2)
        pk.read(4, tbe3)
        pk.read(4, tbe4)
        DBG_dump("scales", sc11[0:1, :], [S.hard(tbe3)])

        wq_ticks = []
        for kc in range(16):
            def fq(h, kc=kc):
                return h.activation(out=wq_sl(kc, 0, 2048), in_=fw_ch(kc),
                                    func=AF.Copy, scale=wsinv_c[:, :])
            dq = [fw_loads[kc], S.hard(tbe), S.hard(wo_mm_last), t_opart]
            wq_ticks.append(S.op("scalar", fq, dq, inc="ak"))

        x2T_ticks = {}
        tr_dep = [S.hard(wo_mm_last), x2_done, S.hard(ofm_scaled_last)]
        for dc in range(16):
            bank = 6 + (dc % 2)
            bdeps = pk.acq([bank]) + tr_dep + [t_ident]
            tt = None
            for lc in range(8):
                def ft(h, dc=dc, lc=lc, bank=bank):
                    return h.transpose(pk.sl(bank, lc * 128, 128, BF16),
                                       x2_tile(lc, dc), ident[:, :])
                tt = S.op("tensor", ft, bdeps if lc == 0 else [],
                          inc="tk" if lc == 7 else None)
            pk.wrote(bank, tt)

            def fe(h, dc=dc, bank=bank):
                return h.tensor_copy(x2T_chunk(dc), pk.sl(bank, 0, 1024, BF16))
            te = S.op("vector", fe, [tt], inc="vk")
            pk.read(bank, te)
            x2T_ticks[dc] = te

        xq_ticks = []
        for dc in range(16):
            def fq(h, dc=dc):
                return h.activation(out=as_b(dc * 1024, 1024, FP8), in_=x2T_chunk(dc),
                                    func=AF.Copy, scale=xsinv_c[:, :])
            dq = [x2T_ticks[dc], S.hard(tbe2)]
            if dc == 0:
                dq += [xres_guard[0], xres_guard[1], x2_done,
                       S.hard(x2_ticks[LC - 2])]
            xq_ticks.append(S.op("scalar", fq, dq, inc="ak"))

        tbias = S.dma("sync", lambda h: h.dma_start(out=as_b(16384, 2048),
                                                    in_=fbias_h[:, :]),
                      [x2_done])

        ybf = as_b(20480, 2048)
        r2_s = as_b(24576, 2048, F32)
        out_dma_last = None
        prev_ty = None
        ln2_q = []
        out_dmas = []
        for lc in range(LC):
            bs = [0, 1, 2, 3] if lc % 2 == 0 else [4, 5, 6, 7]
            if lc >= 1:
                out_dmas.append(ln2_q[lc - 1]())
            deps = pk.acq(bs) + [S.hard(xq_ticks[-1]), S.hard(wq_ticks[-1])]
            tmm = None
            for ct in range(4):
                for kc in range(16):
                    def fmf(h, lc=lc, ct=ct, kc=kc, bs=bs):
                        return h.matmul(
                            pk.sl(bs[ct], 0, 512),
                            as_b(kc * 1024, 1024, FP8)[:, lc * 128:(lc + 1) * 128],
                            wq_sl(kc, ct * 512, 512),
                            start=(kc == 0), stop=(kc == 15))
                    tmm = S.op("tensor", fmf, deps if (ct == 0 and kc == 0) else [],
                               inc="tk" if (ct == 3 and kc == 15) else None)
            for b_ in bs:
                pk.wrote(b_, tmm)
            ywar = [S.hard(prev_ty)] if prev_ty is not None else []
            r2war = [out_dmas[lc - 1]] if lc >= 1 else []
            ty = S.op("scalar", lambda h, bs0=bs[0]: h.activation(
                out=r2_s.rearrange("p (a c) -> p a c", c=512), in_=pk.multi(bs0, 4),
                func=AF.Identity, scale=xs_c[:, :]),
                [tmm, S.hard(tbe4)] + ywar + r2war, inc="ak")
            for b_ in bs:
                pk.read(b_, ty)
            tyw = S.op("vector", lambda h: h.tensor_scalar(
                ybf[:, :], r2_s[:, :], ws_c[:, :], None, op0=ALU.mult),
                [ty, S.hard(tbe3)], inc="vk")
            tyb = S.op("vector", lambda h: h.tensor_tensor(ybf[:, :], ybf[:, :],
                                                           as_b(16384, 2048),
                                                           op=ALU.add),
                       [tyw, tbias], inc="vk")
            if lc == 4:
                DBG_dump("ybf4", ybf[:, :], [S.hard(tyb)])
            tr2 = S.op("vector", lambda h, lc=lc: h.tensor_tensor(
                r2_s.rearrange("p (a c) -> p a c", c=1024), x2_sl(lc),
                ybf.rearrange("p (a c) -> p a c", c=1024), op=ALU.add),
                [tyb, S.hard(x2_ticks[lc])], inc="vk")
            if lc == 4:
                DBG_dump("r2c4", r2_s[:, :], [tr2])
            prev_ty = tr2
            sl_ = lc % 2
            tsum = S.op("vector", lambda h, sl_=sl_: h.tensor_scalar(
                r2_s[:, :], r2_s[:, :], 1.0, None, op0=ALU.mult,
                op1=ALU.add, accum_out=mv[:, 2 * sl_:2 * sl_ + 1]),
                [tr2], inc="vk")
            tsq_ = S.op("vector", lambda h, sl_=sl_: h.scalar_tensor_tensor(
                ybf[:, :], r2_s[:, :], 1.0, r2_s[:, :],
                op0=ALU.mult, op1=ALU.mult,
                accum_out=mv[:, 2 * sl_ + 1:2 * sl_ + 2]), [tsum], inc="vk")
            tm_ = S.op("scalar", lambda h, sl_=sl_: h.activation(
                out=mv[:, 2 * sl_:2 * sl_ + 1], in_=mv[:, 2 * sl_:2 * sl_ + 1],
                func=AF.Identity, scale=f32(1.0 / 2048.0)), [tsq_], inc="ak")
            tms = S.op("scalar", lambda h, sl_=sl_: h.activation(
                out=msq[:, sl_:sl_ + 1], in_=mv[:, 2 * sl_:2 * sl_ + 1],
                func=AF.Square), [tm_], inc="ak")
            tvv = S.op("scalar", lambda h, sl_=sl_: h.activation(
                out=mv[:, 2 * sl_ + 1:2 * sl_ + 2],
                in_=mv[:, 2 * sl_ + 1:2 * sl_ + 2],
                func=AF.Identity, scale=f32(1.0 / 2048.0)), [tms], inc="ak")
            tag2 = S.op("vector", lambda h, sl_=sl_: h.tensor_tensor(
                mv[:, 2 * sl_ + 1:2 * sl_ + 2], mv[:, 2 * sl_ + 1:2 * sl_ + 2],
                msq[:, sl_:sl_ + 1], op=ALU.subtract), [S.hard(tvv)], inc="vk")
            tl1 = S.op("scalar", lambda h, sl_=sl_: h.activation(
                out=isig[:, sl_:sl_ + 1], in_=mv[:, 2 * sl_ + 1:2 * sl_ + 2],
                func=AF.Ln, bias=f32(1e-5)), [tag2], inc="ak")
            tl2 = S.op("scalar", lambda h, sl_=sl_: h.activation(
                out=isig[:, sl_:sl_ + 1], in_=isig[:, sl_:sl_ + 1], func=AF.Exp,
                scale=f32(-0.5)), [tl1], inc="ak")
            tnbA = S.op("scalar", lambda h, sl_=sl_: h.activation(
                out=nbias[:, sl_:sl_ + 1], in_=mv[:, 2 * sl_:2 * sl_ + 1],
                func=AF.Identity, scale=isig[:, sl_:sl_ + 1]), [tl2], inc="ak")
            tnb2 = S.op("scalar", lambda h, sl_=sl_: h.activation(
                out=nbias[:, sl_:sl_ + 1], in_=nbias[:, sl_:sl_ + 1],
                func=AF.Identity, scale=np.float32(-1.0)), [tnbA], inc="ak")

            def emit_out(lc=lc, sl_=sl_, tr2t=tr2, tnbt=tnb2):
                S.op("scalar", lambda h: h.activation(
                    out=tmpc[:, 1:2], in_=tmpc[:, 1:2], func=AF.Identity), [],
                    inc="ak")

                def fap(h):
                    return h.activation(out=r2_s[:, :], in_=r2_s[:, :],
                                        func=AF.Identity, scale=isig[:, sl_:sl_ + 1],
                                        bias=nbias[:, sl_:sl_ + 1])
                ta = S.op("scalar", fap, [tr2t, tnbt], inc="ak")

                def fod(h):
                    return h.dma_start(out=out_h[lc * 128:(lc + 1) * 128, :],
                                       in_=r2_s[:, :])
                return S.dma("sync", fod, [ta])

            ln2_q.append(emit_out)
        S.op("scalar", lambda h: h.activation(out=tmpc[:, 0:1], in_=tmpc[:, 0:1],
                                              func=AF.Identity), [], inc="ak")
        out_dmas.append(ln2_q[LC - 1]())

        if "canary" in DEBUG:
            def fcan(h):
                return h.dma_start(out=dbg["canary"][:, 0:16], in_=canary1[:, :])
            S.dma("sync", fcan, [out_dmas[-1]])
            def fcan2(h):
                return h.dma_start(out=dbg["canary"][:, 16:32], in_=canary2[:, :])
            S.dma("sync", fcan2, [])
            def fcan3(h):
                return h.dma_start(out=dbg["canary"][:, 32:48], in_=canary3[:, :])
            S.dma("sync", fcan3, [])

        S.op("sync", lambda h: h.nop(), [out_dmas[-1]])

        sems = {}
        with contextlib.ExitStack() as st2:
            for name in S.semnames:
                sems[name] = st2.enter_context(nc.semaphore(name))
            blk = st2.enter_context(nc.Block())

            @blk.tensor
            def _(h):
                S.play("tensor", h, sems)

            @blk.vector
            def _(h):
                S.play("vector", h, sems)

            @blk.scalar
            def _(h):
                S.play("scalar", h, sems)

            @blk.sync
            def _(h):
                S.play("sync", h, sems)

            @blk.gpsimd
            def _(h):
                S.play("gpsimd", h, sems)

    return nc


def stage_inputs(inputs):
    x = np.asarray(inputs["L0_attn_x"])
    Wqkv = np.asarray(inputs["L0_attn_Wqkv"])
    Wo = np.asarray(inputs["L0_attn_Wo"])
    Wlr = np.asarray(inputs["L0_attn_Wlr"])
    w0 = np.asarray(inputs["L0_attn_w0"])
    w1 = np.asarray(inputs["L0_attn_w1"])
    w2 = np.asarray(inputs["L0_attn_w2"])
    osc = np.asarray(inputs["L0_attn_o_scale"])
    fw = np.asarray(inputs["L0_ffn_weight"])
    fb = np.asarray(inputs["L0_ffn_bias"])

    fwT = np.ascontiguousarray(fw.T)
    fbias = np.ascontiguousarray(np.broadcast_to(fb[None, :], (128, D)))
    identm = np.eye(128, dtype=ml_dtypes.bfloat16)
    osc_c = np.ascontiguousarray(osc.reshape(128, 1).astype(np.float32))

    maps = []
    for c in range(N_CORES):
        b, p = divmod(c, 2)
        my = slice(p * 1024, p * 1024 + 1024)
        ot = slice((1 - p) * 1024, (1 - p) * 1024 + 1024)
        hs = list(range(8 * p, 8 * p + 8))
        hp = list(range(8 * (1 - p), 8 * (1 - p) + 8))

        xb = x[b]
        xT = np.ascontiguousarray(np.concatenate([xb[my].T, xb[ot].T], axis=1))
        xres = np.ascontiguousarray(xb[my])
        cols = []
        for comp in range(3):
            for hh in hs:
                j = comp * 16 + hh
                cols.append(Wqkv[:, j * 128:(j + 1) * 128])
        lr_cols = Wlr[:, [comp * 16 + hh for comp in range(3) for hh in hs]]
        wcat = np.ascontiguousarray(np.concatenate(cols + [lr_cols], axis=1))
        wo = np.ascontiguousarray(np.concatenate(
            [Wo[hh * 128:(hh + 1) * 128] for hh in hs + hp], axis=0))
        w012t = np.ascontiguousarray(np.concatenate(
            [np.concatenate([w0[hh].T, w1[hh].T, w2[hh].T], axis=1) for hh in hs],
            axis=0))
        w012n = np.ascontiguousarray(np.concatenate(
            [np.concatenate([w0[hh], w1[hh], w2[hh]], axis=1) for hh in hs], axis=0))
        maps.append({
            "xT": xT.astype(np.float32), "xres": xres.astype(np.float32),
            "wcat": wcat.astype(np.float32), "wo": wo.astype(np.float32),
            "w012t": w012t.astype(np.float32), "w012n": w012n.astype(np.float32),
            "osc": osc_c, "fwT": fwT, "fbias": fbias, "identm": identm,
        })
    return maps


def kernel(**inputs):
    maps = stage_inputs(inputs)
    if _NC_CACHE[0] is None:
        _NC_CACHE[0] = build_program()
    nc = _NC_CACHE[0]
    res = run_bass_kernel_spmd(nc, maps, core_ids=list(range(N_CORES)))
    out = np.empty((B, L, D), dtype=np.float32)
    for c in range(N_CORES):
        b, p = divmod(c, 2)
        out[b, p * 1024:(p + 1) * 1024] = res.results[c]["out"]
    if DEBUG:
        kernel.debug_results = res.results
    return out


# version 03
# speedup vs baseline: 1.0009x; 1.0009x over previous
import math
import contextlib
import numpy as np
import ml_dtypes
import concourse.bass as bass
import concourse.mybir as mybir
from concourse.bass_utils import run_bass_kernel_spmd

F32 = mybir.dt.float32
BF16 = mybir.dt.bfloat16
FP8 = mybir.dt.float8e4
AF = mybir.ActivationFunctionType
ALU = mybir.AluOpType
AX = mybir.AxisListType

B, L, D = 4, 2048, 2048
NU, TC, DC, LC = 8, 16, 16, 8
N_CORES = 8

BASE_LR_INV = float(np.float32(0.01 + math.log(-math.expm1(-0.01))))
NS_COEFFS = [(4.0848, -6.8946, 2.927), (3.9505, -6.3029, 2.6377),
             (3.7418, -5.5913, 2.3037), (2.8769, -3.1427, 1.2046),
             (2.8366, -3.0525, 1.2012)]

ENGINES = ("tensor", "vector", "scalar", "sync", "gpsimd")
f32 = np.float32

DEBUG = set()
BARRIERS = False


class Sched:
    def __init__(self):
        self.q = {e: [] for e in ENGINES}
        self.counts = {}
        self.last_wait = {e: {} for e in ENGINES}
        self.semnames = []

    def sem(self, name):
        self.semnames.append(name)
        self.counts[name] = 0

    def op(self, eng, fn, deps=(), inc=None, inc_by=1):
        waits = []
        for t in deps:
            if t is None:
                continue
            peng, s, v = t
            if peng == eng:
                continue
            lw = self.last_wait[eng].get(s, 0)
            if v > lw:
                waits.append((s, v))
                self.last_wait[eng][s] = v
        tick = None
        if inc is not None:
            self.counts[inc] += inc_by
            tick = (eng, inc, self.counts[inc])
        self.q[eng].append((waits, fn, inc, inc_by))
        return tick

    def dma(self, eng, fn, deps=(), sem=None):
        if sem is None:
            sem = "dg" if eng == "gpsimd" else "ds"
        t = self.op(eng, fn, deps, inc=sem, inc_by=16)
        return (None, t[1], t[2])

    def hard(self, tick):
        return None if tick is None else (None, tick[1], tick[2])

    def barrier(self):
        last = {}
        for e in ENGINES:
            for i in range(len(self.q[e]) - 1, -1, -1):
                waits, fn, inc, inc_by = self.q[e][i]
                if inc is not None:
                    last[e] = (None, inc, self.counts[inc])
                    break
        deps = list(last.values())
        outs = {}
        for e in ENGINES:
            if e in ("tensor", "vector", "scalar"):
                outs[e] = self.op(e, lambda h: h.nop(), deps, inc={"tensor": "tk",
                                  "vector": "vk", "scalar": "ak"}[e])
        ticks = [self.hard(t) for t in outs.values()]
        for e in ENGINES:
            self.op(e, lambda h: None, ticks)
        return ticks

    def play(self, eng, h, sems):
        for waits, fn, inc, inc_by in self.q[eng]:
            for s, v in waits:
                h.wait_ge(sems[s], v)
            r = fn(h)
            if inc is not None:
                if r is None:
                    r = h.nop()
                r.then_inc(sems[inc], inc_by)


class Banks:
    def __init__(self, ps):
        self.ps = ps
        self.wr = [None] * 8
        self.rd = [[] for _ in range(8)]

    def sl(self, b, off=0, n=512, dtype=F32):
        ap = self.ps[:, 512 * b: 512 * (b + 1)]
        if dtype != F32:
            ap = ap.bitcast(dtype)
        return ap[:, off:off + n]

    def multi(self, b0, nb, off=0, n=512):
        ap = self.ps[:, 512 * b0: 512 * (b0 + nb)]
        return ap.rearrange("p (b c) -> p b c", c=512)[:, :, off:off + n]

    def acquire(self, b):
        deps = list(self.rd[b])
        if self.wr[b] is not None:
            deps.append(self.wr[b])
        self.rd[b] = []
        return deps

    def acq(self, bs):
        deps = []
        for b in bs:
            deps += self.acquire(b)
        return deps

    def wrote(self, b, tick):
        self.wr[b] = tick

    def read(self, b, tick):
        self.rd[b].append(tick)


_NC_CACHE = [None]


def build_program():
    nc = bass.Bass("TRN2", target_bir_lowering=False, debug=False)

    xT_h = nc.dram_tensor("xT", [D, L], F32, kind="ExternalInput").ap()
    xres_h = nc.dram_tensor("xres", [L // 2, D], F32, kind="ExternalInput").ap()
    wcat_h = nc.dram_tensor("wcat", [D, 3096], F32, kind="ExternalInput").ap()
    wo_h = nc.dram_tensor("wo", [D, D], F32, kind="ExternalInput").ap()
    w012t_h = nc.dram_tensor("w012t", [NU * 128, 384], F32, kind="ExternalInput").ap()
    w012n_h = nc.dram_tensor("w012n", [NU * 128, 384], F32, kind="ExternalInput").ap()
    osc_h = nc.dram_tensor("osc", [128, 1], F32, kind="ExternalInput").ap()
    fwT_h = nc.dram_tensor("fwT", [D, D], BF16, kind="ExternalInput").ap()
    fbias_h = nc.dram_tensor("fbias", [128, D], BF16, kind="ExternalInput").ap()
    ident_h = nc.dram_tensor("identm", [128, 128], BF16, kind="ExternalInput").ap()
    out_h = nc.dram_tensor("out", [L // 2, D], F32, kind="ExternalOutput").ap()

    _DBG_SPECS = {
        "qkv": ([128, TC * 3096], BF16),
        "inv": ([128, TC * 16], F32),
        "lr": ([128, TC * 24], F32),
        "a123": ([128, TC * 384], BF16),
        "rhs": ([128, TC * 256], BF16),
        "rhs1": ([128, TC * 128], BF16),
        "dwP": ([128, 24 * 128], BF16),
        "dwO": ([128, 24 * 128], BF16),
        "nsX": ([128, 24 * 128], BF16),
        "wfin": ([128, 24 * 128], BF16),
        "s_all": ([128, 24], F32),
        "rn_raw": ([128, 24], F32),
        "wn0": ([128, 24], F32),
        "ofm": ([128, TC * 1024], BF16),
        "opart": ([128, NU * 1024], BF16),
        "x2": ([128, TC * 1024], BF16),
        "r1c6": ([128, 2048], F32),
        "lnv6": ([128, 8], F32),
        "canary": ([128, 48], F32),
        "xmloc": ([128, 1], F32),
        "ybf4": ([128, 2048], BF16),
        "r2c4": ([128, 2048], F32),
        "scales": ([1, 6], F32),
    }
    dbg = {}
    for name in DEBUG:
        shape, dt = _DBG_SPECS[name]
        dbg[name] = nc.dram_tensor("dbg_" + name, shape, dt, kind="ExternalOutput").ap()

    ag_in = nc.dram_tensor("ag_in", [NU * 128, 1024], BF16)
    ag_out = nc.dram_tensor("ag_out", [2 * NU * 128, 1024], BF16)
    ar_in = nc.dram_tensor("ar_in", [1, 1], F32)
    ar_out = nc.dram_tensor("ar_out", [1, 1], F32)
    col_d = nc.dram_tensor("col_d", [128], F32)
    col_d2 = nc.dram_tensor("col_d2", [128], F32)
    nrow_d = nc.dram_tensor("nrow_d", [NU, 2048], F32)
    rrow_d = nc.dram_tensor("rrow_d", [NU, 2048], BF16)

    S = Sched()
    for s in ("dg", "dgW", "dgF", "ds", "dsW", "dsF", "tk", "vk", "ak", "cc"):
        S.sem(s)

    with contextlib.ExitStack() as ST:
        def sbuf(name, shape, dt):
            return ST.enter_context(nc.sbuf_tensor(name, shape, dt))

        A_qkv = sbuf("A_qkv", [128, TC * 3096], BF16)
        A_w = sbuf("A_w", [128, 16 * 2048], BF16)
        A_s = sbuf("A_s", [128, 18432], BF16)
        ident = sbuf("identb", [128, 128], BF16)
        ones_col = sbuf("ones_col", [128, 1], BF16)
        ones_row = sbuf("ones_row", [1, 128], BF16)
        ones_row_f = sbuf("ones_row_f", [1, 128], F32)
        osc_s = sbuf("osc_s", [128, 1], F32)
        inv_all = sbuf("inv_all", [128, TC * 16], F32)
        ssum_all = sbuf("ssum_all", [128, TC * 16], F32)
        lr_all = sbuf("lr_all", [128, TC * 24], F32)
        frob_row = sbuf("frob_row", [1, 24], F32)
        invf_col = sbuf("invf_col", [128, 24], F32)
        wnorm0 = sbuf("wnorm0", [128, 24], F32)
        rn_all = sbuf("rn_all", [128, 24], F32)
        s_all = sbuf("s_all", [128, 24], F32)
        s1c = sbuf("s1c", [128, NU], F32)
        mv = sbuf("mv", [128, 4], F32)
        bnst = sbuf("bnst", [128, 24], F32)
        canary1 = sbuf("canary1", [128, 16], F32)
        isig = sbuf("isig", [128, 2], F32)
        msq = sbuf("msq", [128, 2], F32)
        canary2 = sbuf("canary2", [128, 16], F32)
        nbias = sbuf("nbias", [128, 2], F32)
        canary3 = sbuf("canary3", [128, 16], F32)
        xmax = sbuf("xmax", [128, 1], F32)
        xmax8 = sbuf("xmax8", [128, 8], F32)
        wmax16 = sbuf("wmax16", [128, 16], F32)
        wmax = sbuf("wmax", [128, 1], F32)
        tmpc = sbuf("tmpc", [128, 2], F32)
        tmpcb = sbuf("tmpcb", [128, 1], BF16)
        tmpcb24 = sbuf("tmpcb24", [128, 24], BF16)
        row128 = sbuf("row128", [1, 128], F32)
        row128b = sbuf("row128b", [1, 128], F32)
        sc11 = sbuf("sc11", [1, 6], F32)
        xsinv_c = sbuf("xsinv_c", [128, 1], F32)
        dbg_lnv_stage = sbuf("dbg_lnv", [128, 8], F32)
        wsinv_c = sbuf("wsinv_c", [128, 1], F32)
        xw_c = sbuf("xw_c", [128, 1], F32)
        ws_c = sbuf("ws_c", [128, 1], F32)
        xs_c = sbuf("xs_c", [128, 1], F32)
        PS = ST.enter_context(nc.psum_tensor("PS", [128, 4096], F32))
        pk = Banks(PS)

        qkv3 = A_qkv.ap().rearrange("p (t c) -> p t c", c=3096)

        def q_sl(u, tc):
            return qkv3[:, tc, u * 128:(u + 1) * 128]

        def k_sl(u, tc):
            return qkv3[:, tc, 1024 + u * 128:1024 + (u + 1) * 128]

        def v_sl(u, tc):
            return qkv3[:, tc, 2048 + u * 128:2048 + (u + 1) * 128]

        ofm_sl = q_sl

        def ofm_v(u):
            return qkv3[:, :, u * 128:(u + 1) * 128]

        def qfm_half(u, half):
            return qkv3[:, half * 8:(half + 1) * 8, 2048 + u * 128:2048 + (u + 1) * 128]

        def qfm_mov(u, tc0, ntc):
            return qkv3[:, tc0:tc0 + ntc, 2048 + u * 128:2048 + (u + 1) * 128]

        def x2_sl(lc):
            return qkv3[:, 2 * lc:2 * lc + 2, 1024:2048]

        def x2_tile(lc, dcl):
            h, c = divmod(dcl, 8)
            return qkv3[:, 2 * lc + h, 1024 + c * 128:1024 + (c + 1) * 128]

        def x2T_tile(dc, lc):
            if dc < 8:
                return qkv3[:, lc, dc * 128:(dc + 1) * 128]
            return qkv3[:, 8 + lc, (dc - 8) * 128:(dc - 8 + 1) * 128]

        def x2T_chunk(dc):
            if dc < 8:
                return qkv3[:, 0:8, dc * 128:(dc + 1) * 128]
            return qkv3[:, 8:16, (dc - 8) * 128:(dc - 8 + 1) * 128]

        qkvf8 = A_qkv.ap().bitcast(FP8)

        def wq_sl(kc, a, n):
            base = kc * 6192 + 4096
            return qkvf8[:, base + a: base + a + n]

        opart_3d = qkv3[:, 0:8, 2048:3072]
        c1_3d = qkv3[:, 0:8, 1024:2048]
        mine_3d = qkv3[:, 8:16, 0:1024]
        opart4 = qkv3[:, 0:8, 2048:3072].rearrange("p l (j t) -> p l j t", t=128)
        c1_4 = qkv3[:, 0:8, 1024:2048].rearrange("p l (j t) -> p l j t", t=128)

        def opart_sl(j, lc):
            return qkv3[:, lc, 2048 + j * 128:2048 + (j + 1) * 128]

        awb = A_w.ap()

        def aw_b(off_b, n, dt=BF16):
            ap = awb if dt == BF16 else awb.bitcast(dt)
            esz = {BF16: 2, F32: 4, FP8: 1}[dt]
            return ap[:, off_b // esz: off_b // esz + n]

        W_OFF = {"dwP": 0, "dwO": 6144, "nsX": 12288, "nsY": 18432,
                 "nsA": 24576, "nsB": 30720, "w012t": 36864, "w012n": 43008}

        def wmat(nm, m, n=128):
            return aw_b(W_OFF[nm] + m * 256, n)

        def wmat3(nm, u, n=384):
            return aw_b(W_OFF[nm] + u * 768, n)

        def fw_ch(kc):
            return A_w[:, kc * 2048:(kc + 1) * 2048]

        asb = A_s.ap()

        def as_b(off_b, n, dt=BF16):
            ap = asb if dt == BF16 else asb.bitcast(dt)
            esz = {BF16: 2, F32: 4, FP8: 1}[dt]
            return ap[:, off_b // esz: off_b // esz + n]

        def DBG_dump(name, src_ap, deps):
            if name in DEBUG:
                def f(h, src=src_ap):
                    return h.dma_start(out=dbg[name][:, :], in_=src)
                return S.dma("sync", f, deps)
            return None

        t_ident = S.dma("gpsimd", lambda h: h.dma_start(out=ident[:, :], in_=ident_h[:, :]))
        t_osc = S.dma("gpsimd", lambda h: h.dma_start(out=osc_s[:, :], in_=osc_h[:, :]))
        S.op("vector", lambda h: h.memset(ones_col[:, :], 1.0))
        S.op("vector", lambda h: h.memset(ones_row[:, :], 1.0))
        S.op("vector", lambda h: h.memset(ones_row_f[:, :], 1.0), inc="vk")
        S.op("vector", lambda h: h.memset(canary1[:, :], 123.0))
        S.op("vector", lambda h: h.memset(canary2[:, :], 123.0))
        S.op("vector", lambda h: h.memset(canary3[:, :], 123.0), inc="vk")

        xt = [as_b(0, 2048), as_b(4096, 2048)]
        sqs = as_b(8192, 2048)
        wsq = as_b(32768, 2048)

        passes = [(0, 2048, [(0, 512), (512, 1024), (1024, 1536), (1536, 2048)]),
                  (2048, 3096, [(2048, 2560), (2560, 3072), (3072, 3096)])]

        wload = [[None] * DC, [None] * DC]
        for dc in range(DC):
            def f(h, dc=dc):
                return h.dma_start(out=A_w[:, dc * 2048:(dc + 1) * 2048],
                                   in_=wcat_h[dc * 128:(dc + 1) * 128, 0:2048])
            wload[0][dc] = S.dma("gpsimd", f)

        roti = [0]

        def next_bank(banks=(0, 1, 2, 3, 4, 5)):
            b = banks[roti[0] % len(banks)]
            roti[0] += 1
            return b

        xt_load = [None, None]
        xt_guard = [None, None]
        mm_last = None
        pass0_last = None
        t_sqs_free = None
        v_evac_last = None
        lr_evac_last = None

        for pi, (c0, c1, cts) in enumerate(passes):
            colsp = c1 - c0
            if pi == 1:
                bar = S.hard(pass0_last)
                for dc in range(DC):
                    def f(h, dc=dc):
                        return h.dma_start(out=A_w[:, dc * 1048:(dc + 1) * 1048],
                                           in_=wcat_h[dc * 128:(dc + 1) * 128, 2048:3096])
                    wload[1][dc] = S.dma("gpsimd", f, [bar] if dc == 0 else [])
            for tc in range(TC):
                buf = tc % 2
                g = [S.hard(xt_guard[buf])] if xt_guard[buf] is not None else []
                src = xT_h.rearrange("(dc p) l -> p dc l", p=128)[:, :, tc * 128:(tc + 1) * 128]

                def f(h, d=xt[buf], src=src):
                    return h.dma_start(out=d.rearrange("p (dc l) -> p dc l", l=128), in_=src)
                xt_load[buf] = S.dma("gpsimd", f, g)
                evacs = []
                for (a, b_) in cts:
                    w = b_ - a
                    bank = next_bank()
                    deps = pk.acq([bank]) + [xt_load[buf], wload[pi][DC - 1]]
                    tick = None
                    for dc in range(DC):
                        def fm(h, dc=dc, bank=bank, w=w, a=a, buf=buf, colsp=colsp, c0=c0):
                            return h.matmul(pk.sl(bank, 0, w),
                                            xt[buf][:, dc * 128:(dc + 1) * 128],
                                            A_w[:, dc * colsp + (a - c0):
                                                dc * colsp + (a - c0) + w],
                                            start=(dc == 0), stop=(dc == DC - 1))
                        tick = S.op("tensor", fm, deps if dc == 0 else [],
                                    inc="tk" if dc == DC - 1 else None)
                    mm_last = tick
                    if pi == 0:
                        pass0_last = tick
                    pk.wrote(bank, tick)
                    xt_guard[buf] = tick
                    if a < 3072:
                        def fe(h, bank=bank, w=w, tc=tc, a=a):
                            return h.activation(out=qkv3[:, tc, a:a + w],
                                                in_=pk.sl(bank, 0, w), func=AF.Silu)
                        te = S.op("scalar", fe, [tick], inc="ak")
                        if pi == 1:
                            v_evac_last = te
                    else:
                        def fe(h, bank=bank, w=w, tc=tc):
                            return h.tensor_copy(lr_all[:, tc * 24:(tc + 1) * 24],
                                                 pk.sl(bank, 0, w))
                        te = S.op("vector", fe, [tick], inc="vk")
                        lr_evac_last = te
                    pk.read(bank, te)
                    evacs.append(te)
                if pi == 0:
                    def fs(h, tc=tc):
                        return h.tensor_tensor(sqs[:, 0:2048], qkv3[:, tc, 0:2048],
                                               qkv3[:, tc, 0:2048], op=ALU.mult)
                    t1 = S.op("vector", fs, list(evacs), inc="vk")

                    def fr(h, tc=tc):
                        return h.tensor_reduce(
                            ssum_all[:, tc * 16:(tc + 1) * 16],
                            sqs.rearrange("p (g c) -> p g c", c=128),
                            axis=AX.X, op=ALU.add)
                    t_sqs_free = S.op("vector", fr, [t1], inc="vk")

        t_w012t = S.dma("gpsimd", lambda h: h.dma_start(
            out=wmat3("w012t", 0, NU * 384).rearrange("p (u c) -> p u c", c=384),
            in_=w012t_h.rearrange("(u p) c -> p u c", p=128)), [S.hard(pass0_last)])
        t_w012n = S.dma("gpsimd", lambda h: h.dma_start(
            out=wmat3("w012n", 0, NU * 384).rearrange("p (u c) -> p u c", c=384),
            in_=w012n_h.rearrange("(u p) c -> p u c", p=128)), [S.hard(pass0_last)])

        t_wn = None
        for u in range(NU):
            def f1(h, u=u):
                return h.tensor_tensor(wsq[:, 0:384], wmat3("w012n", u),
                                       wmat3("w012n", u), op=ALU.mult)
            a1t = S.op("vector", f1, [t_w012n], inc="vk")

            def f2(h, u=u):
                return h.tensor_reduce(wnorm0[:, 3 * u:3 * u + 3],
                                       wsq[:, 0:384].rearrange("p (g c) -> p g c", c=128),
                                       axis=AX.X, op=ALU.add)
            t_wn = S.op("vector", f2, [a1t], inc="vk")

        t_sq1 = S.op("scalar", lambda h: h.activation(
            out=inv_all[:, :], in_=ssum_all[:, :], func=AF.Sqrt),
            [t_sqs_free], inc="ak")
        t_sq2 = S.op("scalar", lambda h: h.activation(
            out=wnorm0[:, :], in_=wnorm0[:, :], func=AF.Sqrt), [t_wn], inc="ak")
        t_i1 = S.op("vector", lambda h: h.tensor_scalar(
            inv_all[:, :], inv_all[:, :], 1e-5, None, op0=ALU.add), [t_sq1], inc="vk")
        t_inv = S.op("vector", lambda h: h.reciprocal(inv_all[:, :], inv_all[:, :]),
                     [t_i1], inc="vk")
        t_inv = S.op("vector", lambda h: h.tensor_copy(tmpc[:, 0:1], inv_all[:, 0:1]),
                     [t_inv], inc="vk")
        t_lra = S.op("scalar", lambda h: h.activation(
            out=lr_all[:, :], in_=lr_all[:, :], func=AF.Exp,
            bias=f32(BASE_LR_INV)), [lr_evac_last], inc="ak")
        t_lr = S.op("scalar", lambda h: h.activation(
            out=lr_all[:, :], in_=lr_all[:, :], func=AF.Ln,
            bias=f32(1.0)), [t_lra], inc="ak")

        DBG_dump("inv", inv_all[:, :], [t_inv])
        DBG_dump("lr", lr_all[:, :], [S.hard(t_lr)])

        a123 = as_b(0, TC * 384)
        a123_3 = a123.rearrange("p (t c) -> p t c", c=384)
        sg_s = as_b(12288, 2048)
        sil_s = as_b(16384, 2048)
        rhs_cat = as_b(20480, TC * 256)
        rhs3 = rhs_cat.rearrange("p (t c) -> p t c", c=256)
        kfm = as_b(28672, 2048)
        vfm = as_b(32768, 2048)

        def a1s(tc):
            return a123_3[:, tc, 0:128]

        def a2s(tc):
            return a123_3[:, tc, 128:256]

        def a3s(tc):
            return a123_3[:, tc, 256:384]

        a1v = a123_3[:, :, 0:128]
        a2v = a123_3[:, :, 128:256]
        a3v = a123_3[:, :, 256:384]
        sgv = sg_s.rearrange("p (t c) -> p t c", c=128)
        silv = sil_s.rearrange("p (t c) -> p t c", c=128)

        def lrs(comp, tc, u):
            i = tc * 24 + comp * 8 + u
            return lr_all[:, i:i + 1]

        def invs(isq, tc, u):
            i = tc * 16 + (0 if isq else 8) + u
            return inv_all[:, i:i + 1]

        frob_last = None
        frob_first = True
        l4ready = []

        for u in range(NU):
            tsc = None
            for tc in range(TC):
                def fq(h, u=u, tc=tc):
                    return h.tensor_scalar(q_sl(u, tc), q_sl(u, tc),
                                           invs(True, tc, u), None, op0=ALU.mult)
                def fk(h, u=u, tc=tc):
                    return h.tensor_scalar(k_sl(u, tc), k_sl(u, tc),
                                           invs(False, tc, u), None, op0=ALU.mult)
                S.op("vector", fq, [t_inv] if tc == 0 else [], inc="vk")
                tsc = S.op("vector", fk, [], inc="vk")
            tsc = S.hard(tsc)

            def transpose16(src_fn, dst_fn, deps0):
                outs = []
                for half in range(2):
                    bank = 6 + half
                    bdeps = pk.acq([bank]) + deps0 + [t_ident]
                    tt = None
                    for i in range(8):
                        tc = half * 8 + i
                        def ft(h, bank=bank, i=i, tc=tc, src_fn=src_fn):
                            return h.transpose(pk.sl(bank, i * 128, 128, BF16),
                                               src_fn(tc), ident[:, :])
                        tt = S.op("tensor", ft, bdeps if i == 0 else [],
                                  inc="tk" if i == 7 else None)
                    pk.wrote(bank, tt)

                    def fe(h, bank=bank, half=half, dst_fn=dst_fn):
                        return h.tensor_copy(dst_fn(half), pk.sl(bank, 0, 1024, BF16))
                    te = S.op("vector", fe, [tt], inc="vk")
                    pk.read(bank, te)
                    outs.append(te)
                return outs

            tk_fm = transpose16(lambda tc, u=u: k_sl(u, tc),
                                lambda half: kfm[:, half * 1024:(half + 1) * 1024],
                                [tsc])
            tv_fm = transpose16(lambda tc, u=u: v_sl(u, tc),
                                lambda half: vfm[:, half * 1024:(half + 1) * 1024],
                                [S.hard(v_evac_last), S.hard(t_wn)])

            grp = []
            a_evac_last = None
            for tc in range(TC):
                bank = next_bank((3, 4, 5))
                deps = pk.acq([bank]) + tk_fm + tv_fm + [t_w012t, t_w012n]
                if u == 0 and tc == 0:
                    deps += [t_sqs_free, S.hard(t_lr)]

                def f1(h, u=u, tc=tc, bank=bank):
                    return h.matmul(pk.sl(bank, 0, 128), kfm[:, tc * 128:(tc + 1) * 128],
                                    wmat("w012t", 3 * u + 0), start=True, stop=True)

                def f2(h, u=u, tc=tc, bank=bank):
                    return h.matmul(pk.sl(bank, 128, 128), kfm[:, tc * 128:(tc + 1) * 128],
                                    wmat("w012t", 3 * u + 2), start=True, stop=True)

                def f3(h, u=u, tc=tc, bank=bank):
                    return h.matmul(pk.sl(bank, 256, 128), vfm[:, tc * 128:(tc + 1) * 128],
                                    wmat("w012n", 3 * u + 1), start=True, stop=True)
                S.op("tensor", f1, deps)
                S.op("tensor", f2, [])
                tm = S.op("tensor", f3, [], inc="tk")
                pk.wrote(bank, tm)
                grp.append((bank, tm, tc))
                if len(grp) == 3 or tc == TC - 1:
                    b0 = grp[0][0]
                    nb = len(grp)
                    contig = all(grp[i][0] == b0 + i for i in range(nb))
                    gdeps = [g[1] for g in grp]
                    t0 = grp[0][2]
                    if contig:
                        def fe(h, b0=b0, nb=nb, t0=t0):
                            return h.activation(out=a123_3[:, t0:t0 + nb, :],
                                                in_=pk.multi(b0, nb, 0, 384),
                                                func=AF.Identity)
                        a_evac_last = S.op("scalar", fe, gdeps, inc="ak")
                        for g_ in grp:
                            pk.read(g_[0], a_evac_last)
                    else:
                        for (gb, gt, gtc) in grp:
                            def fe(h, gb=gb, gtc=gtc):
                                return h.activation(out=a123_3[:, gtc, :],
                                                    in_=pk.sl(gb, 0, 384),
                                                    func=AF.Identity)
                            a_evac_last = S.op("scalar", fe, [gt], inc="ak")
                            pk.read(gb, a_evac_last)
                    grp = []

            if u == 0:
                DBG_dump("a123", a123[:, :], [a_evac_last])

            tsg = S.op("scalar", lambda h: h.activation(out=sg_s[:, :], in_=a123_3[:, :, 0:128],
                                                        func=AF.Sigmoid),
                       [a_evac_last], inc="ak")
            tt1 = S.op("vector", lambda h: h.tensor_tensor(silv, a1v, sgv, op=ALU.mult),
                       [tsg], inc="vk")
            tt2 = S.op("vector", lambda h: h.tensor_tensor(a1v, silv, sgv, op=ALU.mult),
                       [tt1], inc="vk")
            tt3 = S.op("vector", lambda h: h.tensor_tensor(a1v, sgv, a1v, op=ALU.subtract),
                       [tt2], inc="vk")
            tds = S.op("vector", lambda h: h.tensor_tensor(a1v, a1v, silv, op=ALU.add),
                       [tt3], inc="vk")
            tdg = S.op("vector", lambda h: h.tensor_tensor(sgv, a3v, a2v, op=ALU.mult),
                       [tds], inc="vk")
            t_r = tdg
            for tc in range(TC):
                def f(h, u=u, tc=tc):
                    return h.scalar_tensor_tensor(
                        rhs3[:, tc, 128:256], a3s(tc), lrs(2, tc, u),
                        silv[:, tc, :], op0=ALU.mult, op1=ALU.mult)
                t_r = S.op("vector", f, [S.hard(t_lr)] if (u == 0 and tc == 0) else [],
                           inc="vk")
            for tc in range(TC):
                def f(h, u=u, tc=tc):
                    return h.scalar_tensor_tensor(
                        a3s(tc), silv[:, tc, :], lrs(1, tc, u),
                        a2s(tc), op0=ALU.mult, op1=ALU.mult)
                t_r = S.op("vector", f, [], inc="vk")
            for tc in range(TC):
                def f(h, u=u, tc=tc):
                    return h.scalar_tensor_tensor(
                        rhs3[:, tc, 0:128], sgv[:, tc, :], lrs(0, tc, u),
                        a1v[:, tc, :], op0=ALU.mult, op1=ALU.mult)
                t_r = S.op("vector", f, [], inc="vk")
            t_r = S.hard(t_r)

            if u == 0:
                DBG_dump("rhs", rhs_cat[:, :], [t_r])
                DBG_dump("rhs1", a3v, [t_r])

            d0 = pk.acq([0]) + [t_r]
            d1 = pk.acq([1])
            tdw0 = tdw1 = None
            for tc in range(TC):
                def f0(h, u=u, tc=tc):
                    return h.matmul(pk.sl(0, 0, 256), k_sl(u, tc), rhs3[:, tc, 0:256],
                                    start=(tc == 0), stop=(tc == TC - 1))

                def fv(h, u=u, tc=tc):
                    return h.matmul(pk.sl(1, 0, 128), v_sl(u, tc), a3s(tc),
                                    start=(tc == 0), stop=(tc == TC - 1))
                tdw0 = S.op("tensor", f0, d0 if tc == 0 else [],
                            inc="tk" if tc == TC - 1 else None)
                tdw1 = S.op("tensor", fv, d1 if tc == 0 else [],
                            inc="tk" if tc == TC - 1 else None)
            pk.wrote(0, tdw0)
            pk.wrote(1, tdw1)

            tq_fm = transpose16(lambda tc, u=u: q_sl(u, tc),
                                lambda half, u=u: qfm_half(u, half),
                                [S.hard(tdw1), S.hard(tdw0)])

            def fe0(h, u=u):
                return h.activation(out=wmat("dwP", 3 * u + 0), in_=pk.sl(0, 0, 128),
                                    func=AF.Identity)

            def fe2(h, u=u):
                return h.activation(out=wmat("dwP", 3 * u + 2), in_=pk.sl(0, 128, 128),
                                    func=AF.Identity)

            def fe1(h, u=u):
                return h.activation(out=wmat("dwP", 3 * u + 1), in_=pk.sl(1, 0, 128),
                                    func=AF.Identity)
            te0 = S.op("scalar", fe0, [tdw0], inc="ak")
            te2 = S.op("scalar", fe2, [], inc="ak")
            te1 = S.op("scalar", fe1, [tdw1], inc="ak")
            pk.read(0, te0)
            pk.read(0, te2)
            pk.read(1, te1)

            bdeps = pk.acq([6]) + [te0, te2, te1, t_ident]
            ttr = None
            for m in range(3):
                def ft(h, u=u, m=m):
                    return h.transpose(pk.sl(6, m * 128, 128, BF16),
                                       wmat("dwP", 3 * u + m), ident[:, :])
                ttr = S.op("tensor", ft, bdeps if m == 0 else [],
                           inc="tk" if m == 2 else None)
            pk.wrote(6, ttr)

            def feo(h, u=u):
                return h.tensor_copy(wmat3("dwO", u), pk.sl(6, 0, 384, BF16))
            teo = S.op("vector", feo, [ttr], inc="vk")
            pk.read(6, teo)

            for m in range(3):
                def fsq(h, u=u, m=m):
                    return h.tensor_tensor(sil_s[:, 0:128], wmat("dwP", 3 * u + m),
                                           wmat("dwP", 3 * u + m), op=ALU.mult)
                tq1 = S.op("vector", fsq, [te0, te1, te2], inc="vk")

                mi = 3 * u + m

                def frd(h, mi=mi):
                    with nc.allow_low_precision(reason="bf16 matmul operand"):
                        return h.tensor_reduce(tmpcb24[:, mi:mi + 1], sil_s[:, 0:128],
                                               axis=AX.X, op=ALU.add)
                tq2 = S.op("vector", frd, [tq1], inc="vk")
                bdeps2 = (pk.acq([2]) if frob_first else []) + [tq2]
                frob_first = False

                def fmm(h, mi=mi):
                    return h.matmul(PS[0:1, 1024 + mi:1024 + mi + 1],
                                    tmpcb24[:, mi:mi + 1], ones_col[:, :],
                                    start=True, stop=True)
                frob_last = S.op("tensor", fmm, bdeps2, inc="tk")
            l4ready.append((te0, te1, te2, teo, tq_fm))
        pk.wrote(2, frob_last)

        DBG_dump("dwP", wmat("dwP", 0, 24 * 128), [l4ready[-1][0], l4ready[-1][1],
                                                   l4ready[-1][2]])
        DBG_dump("dwO", wmat("dwO", 0, 24 * 128), [l4ready[-1][3]])

        tfr = S.op("vector", lambda h: h.tensor_copy(frob_row[:, :], PS[0:1, 1024:1048]),
                   [frob_last], inc="vk")
        pk.read(2, tfr)
        tfs = S.op("scalar", lambda h: h.activation(out=frob_row[:, :], in_=frob_row[:, :],
                                                    func=AF.Sqrt), [tfr], inc="ak")
        tf1 = S.op("vector", lambda h: h.tensor_scalar(frob_row[:, :], frob_row[:, :],
                                                       1e-7, None, op0=ALU.add),
                   [tfs], inc="vk")
        tf2 = S.op("vector", lambda h: h.reciprocal(frob_row[:, :], frob_row[:, :]),
                   [tf1], inc="vk")
        tf2 = S.op("vector", lambda h: h.tensor_copy(tmpc[:, 0:1], ones_col[:, :]),
                   [tf2], inc="vk")
        tf2 = S.op("vector", lambda h: h.tensor_copy(tmpc[:, 1:2], ones_col[:, :]),
                   [tf2], inc="vk")
        bdeps = pk.acq([3]) + [tf2, S.hard(S.op("vector", lambda h: None) or None)] \
            if False else pk.acq([3]) + [tf2]
        tbc = None
        for m in range(24):
            def fmm(h, m=m):
                return h.matmul(PS[:, 1536 + m:1536 + m + 1], ones_row_f[:, :],
                                frob_row[0:1, m:m + 1], start=True, stop=True)
            tbc = S.op("tensor", fmm, bdeps if m == 0 else [],
                       inc="tk" if m == 23 else None)
        pk.wrote(3, tbc)
        tie = S.op("vector", lambda h: h.tensor_copy(invf_col[:, :], PS[:, 1536:1560]),
                   [tbc], inc="vk")
        pk.read(3, tie)
        tgap = S.op("vector", lambda h: h.tensor_copy(tmpcb[:, :], ones_col[:, :]),
                    [], inc="vk")
        tX0 = None
        for m in range(24):
            nat = "dwO" if m % 3 != 1 else "dwP"
            tor = "dwP" if m % 3 != 1 else "dwO"

            def fx(h, m=m, nat=nat):
                return h.tensor_scalar(wmat("nsX", m), wmat(nat, m),
                                       invf_col[:, m:m + 1], None, op0=ALU.mult)

            def fy(h, m=m, tor=tor):
                return h.tensor_scalar(wmat("nsY", m), wmat(tor, m),
                                       invf_col[:, m:m + 1], None, op0=ALU.mult)
            S.op("vector", fx, [tie, tgap] if m == 0 else [], inc="vk")
            tX0 = S.op("vector", fy, [], inc="vk")
        tX0 = S.hard(tX0)

        ns_v_last = [tX0] * NU
        l3_mm_last = None
        for it in range(5):
            a_c, b_c, c_c = NS_COEFFS[it]
            for u in range(NU):
                bA = next_bank((0, 1, 2, 3, 4, 5))
                bB = next_bank((0, 1, 2, 3, 4, 5))
                bC = next_bank((0, 1, 2, 3, 4, 5))
                bD = next_bank((0, 1, 2, 3, 4, 5))
                dA = pk.acq([bA]) + [ns_v_last[u]]
                tA = None
                for m in range(3):
                    def fA(h, u=u, m=m, bA=bA):
                        return h.matmul(pk.sl(bA, m * 128, 128), wmat("nsY", 3 * u + m),
                                        wmat("nsY", 3 * u + m), start=True, stop=True)
                    tA = S.op("tensor", fA, dA if m == 0 else [],
                              inc="tk" if m == 2 else None)
                pk.wrote(bA, tA)

                def fAe(h, u=u, bA=bA):
                    return h.tensor_copy(wmat3("nsA", u), pk.sl(bA, 0, 384, BF16)
                                         if False else pk.sl(bA, 0, 384))
                def fAe2(h, u=u, bA=bA):
                    return h.tensor_copy(wmat3("nsA", u), pk.sl(bA, 0, 384))
                tAe = S.op("vector", fAe2, [tA], inc="vk")
                pk.read(bA, tAe)

                dB = pk.acq([bB]) + [tAe]
                tA2 = None
                for m in range(3):
                    def fA2(h, u=u, m=m, bB=bB):
                        return h.matmul(pk.sl(bB, m * 128, 128), wmat("nsA", 3 * u + m),
                                        wmat("nsA", 3 * u + m), start=True, stop=True)
                    tA2 = S.op("tensor", fA2, dB if m == 0 else [],
                               inc="tk" if m == 2 else None)
                pk.wrote(bB, tA2)
                def fB1(h, u=u, bB=bB, c_c=c_c):
                    return h.tensor_scalar(wmat3("nsB", u), pk.sl(bB, 0, 384),
                                           float(c_c), None, op0=ALU.mult)
                tB1 = S.op("vector", fB1, [tA2], inc="vk")
                pk.read(bB, tB1)

                def fB2(h, u=u, b_c=b_c):
                    return h.scalar_tensor_tensor(wmat3("nsB", u), wmat3("nsA", u),
                                                  float(b_c), wmat3("nsB", u),
                                                  op0=ALU.mult, op1=ALU.add)
                tB2 = S.op("vector", fB2, [tB1], inc="vk")

                dC = pk.acq([bC]) + [tB2]
                tBX = None
                for m in range(3):
                    def fBX(h, u=u, m=m, bC=bC):
                        return h.matmul(pk.sl(bC, m * 128, 128), wmat("nsB", 3 * u + m),
                                        wmat("nsX", 3 * u + m), start=True, stop=True)
                    tBX = S.op("tensor", fBX, dC if m == 0 else [],
                               inc="tk" if m == 2 else None)
                pk.wrote(bC, tBX)
                dD = pk.acq([bD])
                tXTB = None
                for m in range(3):
                    def fXTB(h, u=u, m=m, bD=bD):
                        return h.matmul(pk.sl(bD, m * 128, 128), wmat("nsX", 3 * u + m),
                                        wmat("nsB", 3 * u + m), start=True, stop=True)
                    tXTB = S.op("tensor", fXTB, dD if m == 0 else [],
                                inc="tk" if m == 2 else None)
                pk.wrote(bD, tXTB)
                l3_mm_last = tXTB

                def fXn(h, u=u, bC=bC, a_c=a_c):
                    return h.scalar_tensor_tensor(wmat3("nsX", u), wmat3("nsX", u),
                                                  float(a_c), pk.sl(bC, 0, 384),
                                                  op0=ALU.mult, op1=ALU.add)
                tXn = S.op("vector", fXn, [tBX, tXTB], inc="vk")
                pk.read(bC, tXn)

                def fYn(h, u=u, bD=bD, a_c=a_c):
                    return h.scalar_tensor_tensor(wmat3("nsY", u), wmat3("nsY", u),
                                                  float(a_c), pk.sl(bD, 0, 384),
                                                  op0=ALU.mult, op1=ALU.add)
                tYn = S.op("vector", fYn, [tXn], inc="vk")
                pk.read(bD, tYn)
                ns_v_last[u] = tYn

        DBG_dump("nsX", wmat("nsX", 0, 24 * 128), [ns_v_last[-1]])

        def wfin_m(m, n=128):
            return as_b(28672 + m * 256, n)

        wnsc = as_b(12288, 2048)
        l4_v_last = None
        for u in range(NU):
            for m3 in range(3):
                m = 3 * u + m3
                tor = "dwP" if m3 != 1 else "dwO"
                nat = "dwO" if m3 != 1 else "dwP"

                def fT1(h, u=u, m3=m3, m=m):
                    return h.tensor_tensor(wfin_m(m), wmat("w012t", m),
                                           wmat("nsY", m), op=ALU.add)
                tt1 = S.op("vector", fT1, [ns_v_last[u]], inc="vk")

                def fT2(h, m=m, tor=tor):
                    return h.tensor_tensor(wfin_m(m), wfin_m(m), wmat(tor, m),
                                           op=ALU.add)
                S.op("vector", fT2, [tt1], inc="vk")

                def fN1(h, m=m, m3=m3):
                    return h.tensor_tensor(wnsc[:, m3 * 128:(m3 + 1) * 128],
                                           wmat("w012n", m), wmat("nsX", m), op=ALU.add)
                tn1 = S.op("vector", fN1, [], inc="vk")

                def fN2(h, m=m, m3=m3, nat=nat):
                    return h.tensor_tensor(wnsc[:, m3 * 128:(m3 + 1) * 128],
                                           wnsc[:, m3 * 128:(m3 + 1) * 128],
                                           wmat(nat, m), op=ALU.add)
                S.op("vector", fN2, [tn1], inc="vk")
            def fsq(h, u=u):
                return h.tensor_tensor(wnsc[:, 384:768], wnsc[:, 0:384],
                                       wnsc[:, 0:384], op=ALU.mult)
            tq = S.op("vector", fsq, [], inc="vk")

            def frd(h, u=u):
                return h.tensor_reduce(rn_all[:, 3 * u:3 * u + 3],
                                       wnsc[:, 384:768].rearrange("p (g c) -> p g c", c=128),
                                       axis=AX.X, op=ALU.add)
            l4_v_last = S.op("vector", frd, [tq], inc="vk")

        DBG_dump("rn_raw", rn_all[:, :], [l4_v_last])
        DBG_dump("wn0", wnorm0[:, :], [S.hard(t_sq2)])
        trs = S.op("scalar", lambda h: h.activation(out=rn_all[:, :], in_=rn_all[:, :],
                                                    func=AF.Sqrt), [l4_v_last], inc="ak")
        tr1 = S.op("vector", lambda h: h.tensor_scalar(rn_all[:, :], rn_all[:, :],
                                                       1e-5, None, op0=ALU.add),
                   [trs], inc="vk")
        tr2 = S.op("vector", lambda h: h.reciprocal(rn_all[:, :], rn_all[:, :]),
                   [tr1], inc="vk")
        tr2 = S.op("vector", lambda h: h.tensor_copy(tmpc[:, 0:1], ones_col[:, :]),
                   [tr2], inc="vk")
        tr2 = S.op("vector", lambda h: h.tensor_copy(tmpc[:, 1:2], ones_col[:, :]),
                   [tr2], inc="vk")
        ts_all = S.op("vector", lambda h: h.tensor_tensor(s_all[:, :], rn_all[:, :],
                                                          wnorm0[:, :], op=ALU.mult),
                      [tr2, t_sq2], inc="vk")
        s_mid = s_all.rearrange("p (m c) -> p m c", c=3)[:, :, 1]
        tsg_ = S.op("vector", lambda h: h.tensor_copy(tmpc[:, 0:1], ones_col[:, :]),
                    [ts_all], inc="vk")
        tsg_ = S.op("vector", lambda h: h.tensor_copy(tmpc[:, 1:2], ones_col[:, :]),
                    [tsg_], inc="vk")
        ts1c = S.op("vector", lambda h: h.tensor_scalar(s1c[:, :], s_mid, osc_s[:, :],
                                                        None, op0=ALU.mult),
                    [tsg_, t_osc], inc="vk")
        l4_done = S.hard(ts1c)

        DBG_dump("wfin", wfin_m(0, 24 * 128), [l4_done])
        DBG_dump("s_all", s_all[:, :], [l4_done])

        wo_loads = []
        wo_dep = [l4_done, S.hard(l3_mm_last)]
        for fc in range(16):
            def f(h, fc=fc):
                return h.dma_start(out=A_w[:, fc * 2048:(fc + 1) * 2048],
                                   in_=wo_h[fc * 128:(fc + 1) * 128, :])
            wo_loads.append(S.dma("gpsimd", f, wo_dep if fc == 0 else [], sem="dgW"))

        g_s = as_b(0, 2048)
        h_s = as_b(4096, 2048)
        sg2_s = as_b(8192, 2048)
        sq2_s = as_b(12288, 2048)
        srow = as_b(16384, 2048, F32)[0:1, :]

        l5_mm_last = None
        srow_dmas = []
        ofm_ticks = [None] * NU
        for u in range(NU):
            deps = pk.acq([0, 1, 2, 3]) + [l4_done]
            tg = None
            for j in range(4):
                def fg(h, u=u, j=j):
                    return h.matmul(pk.sl(j, 0, 512), wfin_m(3 * u + 0),
                                    qfm_mov(u, 4 * j, 4), start=True, stop=True)
                tg = S.op("tensor", fg, deps if j == 0 else [],
                          inc="tk" if j == 3 else None)
            for j in range(4):
                pk.wrote(j, tg)
            tsg2 = S.op("scalar", lambda h, u=u: h.activation(
                out=sg2_s[:, :], in_=pk.multi(0, 4), func=AF.Sigmoid,
                scale=s_all[:, 3 * u:3 * u + 1]), [tg, S.hard(ts_all)], inc="ak")
            tgid = S.op("scalar", lambda h, u=u: h.activation(
                out=g_s[:, :], in_=pk.multi(0, 4), func=AF.Identity,
                scale=s_all[:, 3 * u:3 * u + 1]), [], inc="ak")
            for j in range(4):
                pk.read(j, tsg2)
                pk.read(j, tgid)
            deps = pk.acq([4, 5, 6, 7])
            th = None
            for j in range(4):
                def fh(h, u=u, j=j):
                    return h.matmul(pk.sl(4 + j, 0, 512), wfin_m(3 * u + 2),
                                    qfm_mov(u, 4 * j, 4), start=True, stop=True)
                th = S.op("tensor", fh, deps if j == 0 else [],
                          inc="tk" if j == 3 else None)
            for j in range(4):
                pk.wrote(4 + j, th)
            thid = S.op("scalar", lambda h, u=u: h.activation(
                out=h_s[:, :], in_=pk.multi(4, 4), func=AF.Identity,
                scale=s_all[:, 3 * u + 2:3 * u + 3]), [th], inc="ak")
            for j in range(4):
                pk.read(4 + j, thid)
            tgt = S.op("vector", lambda h: h.tensor_tensor(g_s[:, :], g_s[:, :],
                                                           sg2_s[:, :], op=ALU.mult),
                       [tsg2, tgid], inc="vk")
            tgh = S.op("vector", lambda h: h.tensor_tensor(h_s[:, :], g_s[:, :],
                                                           h_s[:, :], op=ALU.mult),
                       [tgt, thid], inc="vk")
            deps = pk.acq([0, 1, 2, 3]) + [tgh]
            to = None
            for j in range(4):
                def fo(h, u=u, j=j):
                    return h.matmul(pk.sl(j, 0, 512), wfin_m(3 * u + 1),
                                    h_s[:, j * 512:(j + 1) * 512], start=True, stop=True)
                to = S.op("tensor", fo, deps if j == 0 else [],
                          inc="tk" if j == 3 else None)
            for j in range(4):
                pk.wrote(j, to)
            l5_mm_last = to
            tofm = S.op("scalar", lambda h, u=u: h.activation(
                out=ofm_v(u), in_=pk.multi(0, 4), func=AF.Identity,
                scale=s1c[:, u:u + 1]), [to, S.hard(ts1c)], inc="ak")
            tsq = S.op("scalar", lambda h, u=u: h.activation(
                out=sq2_s[:, :], in_=pk.multi(0, 4), func=AF.Square,
                scale=s_all[:, 3 * u + 1:3 * u + 2]), [], inc="ak")
            for j in range(4):
                pk.read(j, tofm)
                pk.read(j, tsq)
            ofm_ticks[u] = tofm
            deps = pk.acq([4, 5, 6, 7]) + [tsq]
            tno = None
            for j in range(4):
                def fn(h, j=j):
                    return h.matmul(PS[0:1, (4 + j) * 512:(4 + j) * 512 + 512],
                                    ones_col[:, :], sq2_s[:, j * 512:(j + 1) * 512],
                                    start=True, stop=True)
                tno = S.op("tensor", fn, deps if j == 0 else [],
                           inc="tk" if j == 3 else None)
            for j in range(4):
                pk.wrote(4 + j, tno)
            sdep = [tno] + ([srow_dmas[-1]] if srow_dmas else [])
            tsr = S.op("scalar", lambda h: h.activation(
                out=srow.rearrange("o (b c) -> o b c", c=512),
                in_=pk.multi(4, 4)[0:1], func=AF.Identity), sdep, inc="ak")
            for j in range(4):
                pk.read(4 + j, tsr)

            def fdma(h, u=u):
                return h.dma_start(out=nrow_d[u:u + 1, :], in_=srow)
            tsd = S.dma("sync", fdma, [tsr])
            srow_dmas.append(tsd)

        nrm8 = as_b(28672, 2048, F32)[0:8, :]
        nrm8b = as_b(24576, 2048)[0:8, :]
        rrow = as_b(16384, 2048)[0:1, :]
        rb = as_b(20480, 2048)

        tn8 = S.dma("sync", lambda h: h.dma_start(out=nrm8, in_=nrow_d[:, :]),
                    srow_dmas + [S.hard(l5_mm_last)])
        tln = S.op("scalar", lambda h: h.activation(out=nrm8, in_=nrm8, func=AF.Ln,
                                                    scale=f32(1.0 / 128.0),
                                                    bias=f32(1e-5)), [tn8], inc="ak")
        tex = S.op("scalar", lambda h: h.activation(out=nrm8b, in_=nrm8, func=AF.Exp,
                                                    scale=f32(-0.5)), [tln], inc="ak")
        trd = S.dma("sync", lambda h: h.dma_start(out=rrow_d[:, :], in_=nrm8b), [tex])

        ag_sends = []
        prev_tkm = None
        for u in range(NU):
            rdeps = [trd] if u == 0 else [S.hard(prev_tkm)]
            trl = S.dma("sync", lambda h, u=u: h.dma_start(out=rrow, in_=rrow_d[u:u + 1, :]),
                        rdeps)
            deps = pk.acq([0, 1, 2, 3]) + [trl]
            tkm = None
            for j in range(4):
                def fk(h, j=j):
                    return h.matmul(pk.sl(j, 0, 512), ones_row[:, :],
                                    rrow[0:1, j * 512:(j + 1) * 512],
                                    start=True, stop=True)
                tkm = S.op("tensor", fk, deps if j == 0 else [],
                           inc="tk" if j == 3 else None)
            for j in range(4):
                pk.wrote(j, tkm)
            prev_tkm = tkm
            trb = S.op("vector", lambda h: h.tensor_copy(rb[:, :], pk.multi(0, 4)),
                       [tkm], inc="vk")
            for j in range(4):
                pk.read(j, trb)

            def fsc(h, u=u):
                return h.tensor_tensor(ofm_v(u), ofm_v(u),
                                       rb.rearrange("p (t c) -> p t c", c=128),
                                       op=ALU.mult)
            tsc2 = S.op("vector", fsc, [trb, ofm_ticks[u]], inc="vk")

            def fsend(h, u=u):
                return h.dma_start(out=ag_in[u * 128:(u + 1) * 128, :],
                                   in_=qkv3[:, 8:16, u * 128:(u + 1) * 128])
            ag_sends.append(S.dma("sync", fsend, [S.hard(tsc2)]))
        ofm_scaled_last = tsc2

        tag = S.op("gpsimd", lambda h: h.collective_compute(
            "AllGather", ALU.bypass,
            replica_groups=[[0, 1], [2, 3], [4, 5], [6, 7]],
            ins=[ag_in.ap().opt()], outs=[ag_out.ap().opt()]),
            ag_sends, inc="cc")
        tag = S.hard(tag)

        tc0 = tc1 = None
        for j in range(NU):
            def fc0(h, j=j):
                return h.dma_start(
                    out=qkv3[:, 0:8, 2048 + j * 128:2048 + (j + 1) * 128],
                    in_=ag_out[j * 128:(j + 1) * 128, :].rearrange(
                        "p (l t) -> p l t", t=128))
            tc0 = S.dma("sync", fc0, [tag, S.hard(l5_mm_last)] if j == 0 else [])

            def fc1(h, j=j):
                return h.dma_start(
                    out=qkv3[:, 0:8, 1024 + j * 128:1024 + (j + 1) * 128],
                    in_=ag_out[1024 + j * 128:1024 + (j + 1) * 128, :].rearrange(
                        "p (l t) -> p l t", t=128))
            tc1 = S.dma("sync", fc1, [tag] if j == 0 else [])
        tp1 = S.op("vector", lambda h: h.tensor_tensor(opart_3d, opart_3d, c1_3d,
                                                       op=ALU.add), [tc0, tc1], inc="vk")
        tp2 = S.op("vector", lambda h: h.tensor_tensor(opart_3d, opart_3d, mine_3d,
                                                       op=ALU.subtract),
                   [tp1, S.hard(ofm_scaled_last)], inc="vk")
        t_opart = S.hard(tp2)

        DBG_dump("ofm", qkv3[:, :, 0:1024], [S.hard(ofm_scaled_last)])
        DBG_dump("opart", qkv3[:, 0:8, 2048:3072], [t_opart])

        if BARRIERS:
            S.barrier()
        xres_b = [as_b(0, 2048, F32), as_b(8192, 2048, F32)]
        r1_b = [as_b(16384, 2048, F32), as_b(24576, 2048, F32)]
        sqw = as_b(32768, 2048)
        xres_t = [None, None]
        xres_guard = [None, None]
        ln_q = []
        x2_ticks = [None] * LC
        wo_mm_last = None
        xm_last = None

        for lc in range(LC):
            buf = lc % 2
            if lc >= 1:
                x2_ticks[lc - 1], xm_last = ln_q[lc - 1]()
            gdeps = [xres_guard[buf]] if xres_guard[buf] is not None else []

            def fx(h, lc=lc, buf=buf):
                return h.dma_start(out=xres_b[buf].rearrange("p (a c) -> p a c", c=512)
                                   if False else xres_b[buf][:, :],
                                   in_=xres_h[lc * 128:(lc + 1) * 128, :])
            xres_t[buf] = S.dma("sync", fx, gdeps, sem="dsW")
            bs = [0, 1, 2, 3] if lc % 2 == 0 else [4, 5, 6, 7]
            deps = pk.acq(bs) + [t_opart, S.hard(ofm_scaled_last), wo_loads[-1]]
            tmm = None
            for ct in range(4):
                for fc in range(16):
                    if fc < 8:
                        stat = ofm_sl(fc, lc)
                    else:
                        stat = opart_sl(fc - 8, lc)

                    def fmo(h, stat=stat, ct=ct, fc=fc, bs=bs):
                        return h.matmul(pk.sl(bs[ct], 0, 512), stat,
                                        A_w[:, fc * 2048 + ct * 512:
                                            fc * 2048 + ct * 512 + 512],
                                        start=(fc == 0), stop=(fc == 15))
                    tmm = S.op("tensor", fmo, deps if (ct == 0 and fc == 0) else [],
                               inc="tk" if (ct == 3 and fc == 15) else None)
            for b_ in bs:
                pk.wrote(b_, tmm)
            wo_mm_last = tmm
            def fr1(h, bs0=bs[0], buf=buf):
                return h.tensor_tensor(r1_b[buf].rearrange("p (a c) -> p a c", c=512),
                                       pk.multi(bs0, 4),
                                       xres_b[buf].rearrange("p (a c) -> p a c", c=512),
                                       op=ALU.add)
            r1war = [S.hard(x2_ticks[lc - 2])] if lc >= 2 else []
            tr1_ = S.op("vector", fr1, [tmm, xres_t[buf]] + r1war, inc="vk")
            if lc == 6:
                DBG_dump("r1c6", r1_b[0][:, :], [tr1_])
            for b_ in bs:
                pk.read(b_, tr1_)
            xres_guard[buf] = S.hard(tr1_)
            sl_ = lc % 2
            tsum = S.op("vector", lambda h, buf=buf, sl_=sl_: h.tensor_scalar(
                r1_b[buf][:, :], r1_b[buf][:, :], 1.0, None, op0=ALU.mult,
                op1=ALU.add, accum_out=mv[:, 2 * sl_:2 * sl_ + 1]),
                [tr1_], inc="vk")
            tsq_ = S.op("vector", lambda h, buf=buf, sl_=sl_: h.scalar_tensor_tensor(
                sqw[:, :], r1_b[buf][:, :], 1.0, r1_b[buf][:, :],
                op0=ALU.mult, op1=ALU.mult,
                accum_out=mv[:, 2 * sl_ + 1:2 * sl_ + 2]), [tsum], inc="vk")
            tm_ = S.op("scalar", lambda h, sl_=sl_: h.activation(
                out=mv[:, 2 * sl_:2 * sl_ + 1], in_=mv[:, 2 * sl_:2 * sl_ + 1],
                func=AF.Identity, scale=f32(1.0 / 2048.0)), [tsq_], inc="ak")
            tms = S.op("scalar", lambda h, sl_=sl_: h.activation(
                out=msq[:, sl_:sl_ + 1], in_=mv[:, 2 * sl_:2 * sl_ + 1],
                func=AF.Square), [tm_], inc="ak")
            tvv = S.op("scalar", lambda h, sl_=sl_: h.activation(
                out=mv[:, 2 * sl_ + 1:2 * sl_ + 2],
                in_=mv[:, 2 * sl_ + 1:2 * sl_ + 2],
                func=AF.Identity, scale=f32(1.0 / 2048.0)), [tms], inc="ak")
            tag_ = S.op("vector", lambda h, sl_=sl_: h.tensor_tensor(
                mv[:, 2 * sl_ + 1:2 * sl_ + 2], mv[:, 2 * sl_ + 1:2 * sl_ + 2],
                msq[:, sl_:sl_ + 1], op=ALU.subtract), [S.hard(tvv)], inc="vk")
            tl1 = S.op("scalar", lambda h, sl_=sl_: h.activation(
                out=isig[:, sl_:sl_ + 1], in_=mv[:, 2 * sl_ + 1:2 * sl_ + 2],
                func=AF.Ln, bias=f32(1e-5)), [tag_], inc="ak")
            tl2 = S.op("scalar", lambda h, sl_=sl_: h.activation(
                out=isig[:, sl_:sl_ + 1], in_=isig[:, sl_:sl_ + 1], func=AF.Exp,
                scale=f32(-0.5)), [tl1], inc="ak")
            tnbA = S.op("scalar", lambda h, sl_=sl_: h.activation(
                out=nbias[:, sl_:sl_ + 1], in_=mv[:, 2 * sl_:2 * sl_ + 1],
                func=AF.Identity, scale=isig[:, sl_:sl_ + 1]), [tl2], inc="ak")
            tnb2 = S.op("scalar", lambda h, sl_=sl_: h.activation(
                out=nbias[:, sl_:sl_ + 1], in_=nbias[:, sl_:sl_ + 1],
                func=AF.Identity, scale=np.float32(-1.0)), [tnbA], inc="ak")

            if lc == 6 and "lnv6" in DEBUG:
                def fdump(h, sl_=sl_):
                    return h.tensor_copy(dbg_lnv_stage[:, 0:6],
                                         mv[:, 0:2])
                tcp1 = S.op("vector", lambda h, sl_=sl_: h.tensor_copy(
                    dbg_lnv_stage[:, 0:2], mv[:, 2 * sl_:2 * sl_ + 2]), [tnb2], inc="vk")
                tcp2 = S.op("vector", lambda h, sl_=sl_: h.tensor_copy(
                    dbg_lnv_stage[:, 2:3], isig[:, sl_:sl_ + 1]), [tcp1], inc="vk")
                tcp3 = S.op("vector", lambda h, sl_=sl_: h.tensor_copy(
                    dbg_lnv_stage[:, 3:4], nbias[:, sl_:sl_ + 1]), [tcp2], inc="vk")
                DBG_dump("lnv6", dbg_lnv_stage[:, :], [tcp3])

            def emit_apply(lc=lc, sl_=sl_, buf=buf, tr1t=tr1_, tnbt=tnb2):
                S.op("scalar", lambda h: h.activation(
                    out=tmpc[:, 1:2], in_=tmpc[:, 1:2], func=AF.Identity), [],
                    inc="ak")

                def fap(h):
                    return h.activation(out=x2_sl(lc), in_=r1_b[buf].rearrange(
                        "p (a c) -> p a c", c=1024), func=AF.Identity,
                        scale=isig[:, sl_:sl_ + 1], bias=nbias[:, sl_:sl_ + 1])
                dep_extra = [t_opart] if lc < 4 else []
                ta = S.op("scalar", fap, [tr1t, tnbt] + dep_extra, inc="ak")
                def fxm(h, lc=lc):
                    return h.tensor_reduce(xmax8[:, lc:lc + 1], x2_sl(lc),
                                           axis=AX.XY, op=ALU.max,
                                           apply_absolute_value=True)
                tm2 = S.op("vector", fxm, [ta], inc="vk")
                return ta, tm2

            ln_q.append(emit_apply)
        S.op("scalar", lambda h: h.activation(out=tmpc[:, 0:1], in_=tmpc[:, 0:1],
                                              func=AF.Identity), [], inc="ak")
        x2_ticks[LC - 1], xm_last = ln_q[LC - 1]()
        x2_done = S.hard(x2_ticks[LC - 1])

        DBG_dump("x2", qkv3[:, :, 1024:2048], [x2_done])

        txg = S.op("vector", lambda h: h.tensor_copy(tmpc[:, 0:1], ones_col[:, :]),
                   [xm_last], inc="vk")
        txg = S.op("vector", lambda h: h.tensor_copy(tmpc[:, 1:2], ones_col[:, :]),
                   [txg], inc="vk")
        txr = S.op("vector", lambda h: h.tensor_reduce(
            xmax[:, :], xmax8[:, :], axis=AX.X, op=ALU.max), [txg], inc="vk")
        txr = S.op("vector", lambda h: h.tensor_copy(tmpc[:, 0:1], ones_col[:, :]),
                   [txr], inc="vk")
        txr = S.op("vector", lambda h: h.tensor_copy(tmpc[:, 1:2], ones_col[:, :]),
                   [txr], inc="vk")
        DBG_dump("xmloc", xmax[:, 0:1], [S.hard(txr)])
        tb1 = S.dma("sync", lambda h: h.dma_start(out=col_d.rearrange('(p o) -> p o', o=1), in_=xmax[:, 0:1]),
                    [S.hard(txr)], sem="dsW")
        tb2 = S.dma("sync", lambda h: h.dma_start(
            out=row128[:, :], in_=col_d.rearrange("(o p) -> o p", o=1)), [tb1],
            sem="dsW")
        tb3 = S.op("vector", lambda h: h.tensor_reduce(sc11[0:1, 2:3], row128[:, :],
                                                       axis=AX.X, op=ALU.max),
                   [tb2], inc="vk")
        tb3 = S.op("vector", lambda h: h.tensor_copy(tmpc[:, 0:1], ones_col[:, :]),
                   [tb3], inc="vk")
        tb3 = S.op("vector", lambda h: h.tensor_copy(tmpc[:, 1:2], ones_col[:, :]),
                   [tb3], inc="vk")
        tb4 = S.op("vector", lambda h: h.tensor_scalar(sc11[0:1, 2:3], sc11[0:1, 2:3],
                                                       1e-6, None, op0=ALU.max),
                   [tb3], inc="vk")
        tb4 = S.op("vector", lambda h: h.tensor_copy(tmpc[:, 0:1], ones_col[:, :]),
                   [tb4], inc="vk")
        tb4 = S.op("vector", lambda h: h.tensor_copy(tmpc[:, 1:2], ones_col[:, :]),
                   [tb4], inc="vk")
        tb5 = S.dma("sync", lambda h: h.dma_start(out=ar_in[:, :], in_=sc11[0:1, 2:3]),
                    [S.hard(tb4)], sem="dsW")
        tar = S.op("gpsimd", lambda h: h.collective_compute(
            "AllReduce", ALU.max, replica_groups=[list(range(8))],
            ins=[ar_in.ap().opt()], outs=[ar_out.ap().opt()]), [tb5], inc="cc")
        tar = S.hard(tar)
        txs = S.dma("sync", lambda h: h.dma_start(out=sc11[0:1, 2:3], in_=ar_out[:, :]),
                    [tar], sem="dsW")

        if BARRIERS:
            S.barrier()
        fw_loads = []
        ws_ticks = []
        for kc in range(16):
            def f(h, kc=kc):
                return h.dma_start(out=fw_ch(kc), in_=fwT_h[kc * 128:(kc + 1) * 128, :])
            tl = S.dma("gpsimd", f, [S.hard(wo_mm_last)] if kc == 0 else [], sem="dgF")
            fw_loads.append(tl)

            def fwm(h, kc=kc):
                return h.tensor_reduce(wmax16[:, kc:kc + 1], fw_ch(kc), axis=AX.X,
                                       op=ALU.max, apply_absolute_value=True)
            ws_ticks.append(S.op("vector", fwm, [tl, x2_done], inc="vk"))
        twg = S.op("vector", lambda h: h.tensor_copy(tmpc[:, 0:1], ones_col[:, :]),
                   ws_ticks, inc="vk")
        twg = S.op("vector", lambda h: h.tensor_copy(tmpc[:, 1:2], ones_col[:, :]),
                   [twg], inc="vk")
        twr = S.op("vector", lambda h: h.tensor_reduce(
            wmax[:, :], wmax16[:, :], axis=AX.X, op=ALU.max),
            [twg], inc="vk")
        twr = S.op("vector", lambda h: h.tensor_copy(tmpc[:, 0:1], ones_col[:, :]),
                   [twr], inc="vk")
        twr = S.op("vector", lambda h: h.tensor_copy(tmpc[:, 1:2], ones_col[:, :]),
                   [twr], inc="vk")
        tw1 = S.dma("sync", lambda h: h.dma_start(out=col_d2.rearrange('(p o) -> p o', o=1), in_=wmax[:, 0:1]),
                    [S.hard(twr)], sem="dsF")
        tw2 = S.dma("sync", lambda h: h.dma_start(
            out=row128b[:, :], in_=col_d2.rearrange("(o p) -> o p", o=1)), [tw1],
            sem="dsF")
        tw3 = S.op("vector", lambda h: h.tensor_reduce(sc11[0:1, 0:1], row128b[:, :],
                                                       axis=AX.X, op=ALU.max),
                   [tw2], inc="vk")
        tw3 = S.op("vector", lambda h: h.tensor_copy(tmpc[:, 0:1], ones_col[:, :]),
                   [tw3], inc="vk")
        tw3 = S.op("vector", lambda h: h.tensor_copy(tmpc[:, 1:2], ones_col[:, :]),
                   [tw3], inc="vk")
        tw4 = S.op("vector", lambda h: h.tensor_scalar(sc11[0:1, 0:1], sc11[0:1, 0:1],
                                                       1e-6, None, op0=ALU.max),
                   [tw3], inc="vk")
        tw4 = S.op("vector", lambda h: h.tensor_copy(tmpc[:, 0:1], ones_col[:, :]),
                   [tw4], inc="vk")
        tw5 = S.op("vector", lambda h: h.reciprocal(sc11[0:1, 1:2], sc11[0:1, 0:1]),
                   [tw4], inc="vk")
        tw5 = S.op("vector", lambda h: h.tensor_copy(tmpc[:, 0:1], ones_col[:, :]),
                   [tw5], inc="vk")
        tx1 = S.op("vector", lambda h: h.reciprocal(sc11[0:1, 4:5], sc11[0:1, 2:3]),
                   [txs], inc="vk")
        tx1 = S.op("vector", lambda h: h.tensor_copy(tmpc[:, 1:2], ones_col[:, :]),
                   [tx1], inc="vk")
        tx2 = S.op("vector", lambda h: h.tensor_tensor(sc11[0:1, 3:4], sc11[0:1, 2:3],
                                                       sc11[0:1, 0:1], op=ALU.mult),
                   [tx1, tw5], inc="vk")
        tx2 = S.op("vector", lambda h: h.tensor_copy(tmpc[:, 0:1], ones_col[:, :]),
                   [tx2], inc="vk")
        tx2 = S.op("vector", lambda h: h.tensor_copy(tmpc[:, 1:2], ones_col[:, :]),
                   [tx2], inc="vk")
        bdeps = pk.acq([4]) + [S.hard(tw5), S.hard(tx2)]
        tbw = None
        for i, cell in enumerate((1, 4, 0, 2)):
            def fbc(h, cell=cell, i=i):
                return h.matmul(PS[:, 4 * 512 + i:4 * 512 + i + 1], ones_row_f[:, :],
                                sc11[0:1, cell:cell + 1], start=True, stop=True)
            tbw = S.op("tensor", fbc, bdeps if i == 0 else [],
                       inc="tk" if i == 3 else None)
        pk.wrote(4, tbw)
        tbe = S.op("vector", lambda h: h.tensor_copy(wsinv_c[:, :], PS[:, 2048:2049]),
                   [tbw], inc="vk")
        tbe2 = S.op("vector", lambda h: h.tensor_copy(xsinv_c[:, :], PS[:, 2049:2050]),
                    [], inc="vk")
        tbe3 = S.op("vector", lambda h: h.tensor_copy(ws_c[:, :], PS[:, 2050:2051]),
                    [], inc="vk")
        tbe4 = S.op("vector", lambda h: h.tensor_copy(xs_c[:, :], PS[:, 2051:2052]),
                    [], inc="vk")
        pk.read(4, tbe)
        pk.read(4, tbe2)
        pk.read(4, tbe3)
        pk.read(4, tbe4)
        DBG_dump("scales", sc11[0:1, :], [S.hard(tbe3)])

        wq_ticks = []
        for kc in range(16):
            def fq(h, kc=kc):
                return h.activation(out=wq_sl(kc, 0, 2048), in_=fw_ch(kc),
                                    func=AF.Copy, scale=wsinv_c[:, :])
            dq = [fw_loads[kc], S.hard(tbe), S.hard(wo_mm_last), t_opart]
            wq_ticks.append(S.op("scalar", fq, dq, inc="ak"))

        x2T_ticks = {}
        tr_dep = [S.hard(wo_mm_last), x2_done, S.hard(ofm_scaled_last)]
        for dc in range(16):
            bank = 6 + (dc % 2)
            bdeps = pk.acq([bank]) + tr_dep + [t_ident]
            tt = None
            for lc in range(8):
                def ft(h, dc=dc, lc=lc, bank=bank):
                    return h.transpose(pk.sl(bank, lc * 128, 128, BF16),
                                       x2_tile(lc, dc), ident[:, :])
                tt = S.op("tensor", ft, bdeps if lc == 0 else [],
                          inc="tk" if lc == 7 else None)
            pk.wrote(bank, tt)

            def fe(h, dc=dc, bank=bank):
                return h.tensor_copy(x2T_chunk(dc), pk.sl(bank, 0, 1024, BF16))
            te = S.op("vector", fe, [tt], inc="vk")
            pk.read(bank, te)
            x2T_ticks[dc] = te

        xq_ticks = []
        for dc in range(16):
            def fq(h, dc=dc):
                return h.activation(out=as_b(dc * 1024, 1024, FP8), in_=x2T_chunk(dc),
                                    func=AF.Copy, scale=xsinv_c[:, :])
            dq = [x2T_ticks[dc], S.hard(tbe2)]
            if dc == 0:
                dq += [xres_guard[0], xres_guard[1], x2_done,
                       S.hard(x2_ticks[LC - 2])]
            xq_ticks.append(S.op("scalar", fq, dq, inc="ak"))

        tbias = S.dma("sync", lambda h: h.dma_start(out=as_b(16384, 2048),
                                                    in_=fbias_h[:, :]),
                      [x2_done], sem="dsF")

        ybf = as_b(20480, 2048)
        r2_s = as_b(24576, 2048, F32)
        out_dma_last = None
        prev_ty = None
        ln2_q = []
        out_dmas = []
        for lc in range(LC):
            bs = [0, 1, 2, 3] if lc % 2 == 0 else [4, 5, 6, 7]
            if lc >= 1:
                out_dmas.append(ln2_q[lc - 1]())
            deps = pk.acq(bs) + [S.hard(xq_ticks[-1]), S.hard(wq_ticks[-1])]
            tmm = None
            for ct in range(4):
                for kc in range(16):
                    def fmf(h, lc=lc, ct=ct, kc=kc, bs=bs):
                        return h.matmul(
                            pk.sl(bs[ct], 0, 512),
                            as_b(kc * 1024, 1024, FP8)[:, lc * 128:(lc + 1) * 128],
                            wq_sl(kc, ct * 512, 512),
                            start=(kc == 0), stop=(kc == 15))
                    tmm = S.op("tensor", fmf, deps if (ct == 0 and kc == 0) else [],
                               inc="tk" if (ct == 3 and kc == 15) else None)
            for b_ in bs:
                pk.wrote(b_, tmm)
            ywar = [S.hard(prev_ty)] if prev_ty is not None else []
            r2war = [out_dmas[lc - 1]] if lc >= 1 else []
            ty = S.op("scalar", lambda h, bs0=bs[0]: h.activation(
                out=r2_s.rearrange("p (a c) -> p a c", c=512), in_=pk.multi(bs0, 4),
                func=AF.Identity, scale=xs_c[:, :]),
                [tmm, S.hard(tbe4)] + ywar + r2war, inc="ak")
            for b_ in bs:
                pk.read(b_, ty)
            tyw = S.op("vector", lambda h: h.tensor_scalar(
                ybf[:, :], r2_s[:, :], ws_c[:, :], None, op0=ALU.mult),
                [ty, S.hard(tbe3)], inc="vk")
            tyb = S.op("vector", lambda h: h.tensor_tensor(ybf[:, :], ybf[:, :],
                                                           as_b(16384, 2048),
                                                           op=ALU.add),
                       [tyw, tbias], inc="vk")
            if lc == 4:
                DBG_dump("ybf4", ybf[:, :], [S.hard(tyb)])
            tr2 = S.op("vector", lambda h, lc=lc: h.tensor_tensor(
                r2_s.rearrange("p (a c) -> p a c", c=1024), x2_sl(lc),
                ybf.rearrange("p (a c) -> p a c", c=1024), op=ALU.add),
                [tyb, S.hard(x2_ticks[lc])], inc="vk")
            if lc == 4:
                DBG_dump("r2c4", r2_s[:, :], [tr2])
            prev_ty = tr2
            sl_ = lc % 2
            tsum = S.op("vector", lambda h, sl_=sl_: h.tensor_scalar(
                r2_s[:, :], r2_s[:, :], 1.0, None, op0=ALU.mult,
                op1=ALU.add, accum_out=mv[:, 2 * sl_:2 * sl_ + 1]),
                [tr2], inc="vk")
            tsq_ = S.op("vector", lambda h, sl_=sl_: h.scalar_tensor_tensor(
                ybf[:, :], r2_s[:, :], 1.0, r2_s[:, :],
                op0=ALU.mult, op1=ALU.mult,
                accum_out=mv[:, 2 * sl_ + 1:2 * sl_ + 2]), [tsum], inc="vk")
            tm_ = S.op("scalar", lambda h, sl_=sl_: h.activation(
                out=mv[:, 2 * sl_:2 * sl_ + 1], in_=mv[:, 2 * sl_:2 * sl_ + 1],
                func=AF.Identity, scale=f32(1.0 / 2048.0)), [tsq_], inc="ak")
            tms = S.op("scalar", lambda h, sl_=sl_: h.activation(
                out=msq[:, sl_:sl_ + 1], in_=mv[:, 2 * sl_:2 * sl_ + 1],
                func=AF.Square), [tm_], inc="ak")
            tvv = S.op("scalar", lambda h, sl_=sl_: h.activation(
                out=mv[:, 2 * sl_ + 1:2 * sl_ + 2],
                in_=mv[:, 2 * sl_ + 1:2 * sl_ + 2],
                func=AF.Identity, scale=f32(1.0 / 2048.0)), [tms], inc="ak")
            tag2 = S.op("vector", lambda h, sl_=sl_: h.tensor_tensor(
                mv[:, 2 * sl_ + 1:2 * sl_ + 2], mv[:, 2 * sl_ + 1:2 * sl_ + 2],
                msq[:, sl_:sl_ + 1], op=ALU.subtract), [S.hard(tvv)], inc="vk")
            tl1 = S.op("scalar", lambda h, sl_=sl_: h.activation(
                out=isig[:, sl_:sl_ + 1], in_=mv[:, 2 * sl_ + 1:2 * sl_ + 2],
                func=AF.Ln, bias=f32(1e-5)), [tag2], inc="ak")
            tl2 = S.op("scalar", lambda h, sl_=sl_: h.activation(
                out=isig[:, sl_:sl_ + 1], in_=isig[:, sl_:sl_ + 1], func=AF.Exp,
                scale=f32(-0.5)), [tl1], inc="ak")
            tnbA = S.op("scalar", lambda h, sl_=sl_: h.activation(
                out=nbias[:, sl_:sl_ + 1], in_=mv[:, 2 * sl_:2 * sl_ + 1],
                func=AF.Identity, scale=isig[:, sl_:sl_ + 1]), [tl2], inc="ak")
            tnb2 = S.op("scalar", lambda h, sl_=sl_: h.activation(
                out=nbias[:, sl_:sl_ + 1], in_=nbias[:, sl_:sl_ + 1],
                func=AF.Identity, scale=np.float32(-1.0)), [tnbA], inc="ak")

            def emit_out(lc=lc, sl_=sl_, tr2t=tr2, tnbt=tnb2):
                S.op("scalar", lambda h: h.activation(
                    out=tmpc[:, 1:2], in_=tmpc[:, 1:2], func=AF.Identity), [],
                    inc="ak")

                def fap(h):
                    return h.activation(out=r2_s[:, :], in_=r2_s[:, :],
                                        func=AF.Identity, scale=isig[:, sl_:sl_ + 1],
                                        bias=nbias[:, sl_:sl_ + 1])
                ta = S.op("scalar", fap, [tr2t, tnbt], inc="ak")

                def fod(h):
                    return h.dma_start(out=out_h[lc * 128:(lc + 1) * 128, :],
                                       in_=r2_s[:, :])
                return S.dma("sync", fod, [ta], sem="dsF")

            ln2_q.append(emit_out)
        S.op("scalar", lambda h: h.activation(out=tmpc[:, 0:1], in_=tmpc[:, 0:1],
                                              func=AF.Identity), [], inc="ak")
        out_dmas.append(ln2_q[LC - 1]())

        if "canary" in DEBUG:
            def fcan(h):
                return h.dma_start(out=dbg["canary"][:, 0:16], in_=canary1[:, :])
            S.dma("sync", fcan, [out_dmas[-1]])
            def fcan2(h):
                return h.dma_start(out=dbg["canary"][:, 16:32], in_=canary2[:, :])
            S.dma("sync", fcan2, [])
            def fcan3(h):
                return h.dma_start(out=dbg["canary"][:, 32:48], in_=canary3[:, :])
            S.dma("sync", fcan3, [])

        S.op("sync", lambda h: h.nop(), [out_dmas[-1]])

        sems = {}
        with contextlib.ExitStack() as st2:
            for name in S.semnames:
                sems[name] = st2.enter_context(nc.semaphore(name))
            blk = st2.enter_context(nc.Block())

            @blk.tensor
            def _(h):
                S.play("tensor", h, sems)

            @blk.vector
            def _(h):
                S.play("vector", h, sems)

            @blk.scalar
            def _(h):
                S.play("scalar", h, sems)

            @blk.sync
            def _(h):
                S.play("sync", h, sems)

            @blk.gpsimd
            def _(h):
                S.play("gpsimd", h, sems)

    return nc


def stage_inputs(inputs):
    x = np.asarray(inputs["L0_attn_x"])
    Wqkv = np.asarray(inputs["L0_attn_Wqkv"])
    Wo = np.asarray(inputs["L0_attn_Wo"])
    Wlr = np.asarray(inputs["L0_attn_Wlr"])
    w0 = np.asarray(inputs["L0_attn_w0"])
    w1 = np.asarray(inputs["L0_attn_w1"])
    w2 = np.asarray(inputs["L0_attn_w2"])
    osc = np.asarray(inputs["L0_attn_o_scale"])
    fw = np.asarray(inputs["L0_ffn_weight"])
    fb = np.asarray(inputs["L0_ffn_bias"])

    fwT = np.ascontiguousarray(fw.T)
    fbias = np.ascontiguousarray(np.broadcast_to(fb[None, :], (128, D)))
    identm = np.eye(128, dtype=ml_dtypes.bfloat16)
    osc_c = np.ascontiguousarray(osc.reshape(128, 1).astype(np.float32))

    maps = []
    for c in range(N_CORES):
        b, p = divmod(c, 2)
        my = slice(p * 1024, p * 1024 + 1024)
        ot = slice((1 - p) * 1024, (1 - p) * 1024 + 1024)
        hs = list(range(8 * p, 8 * p + 8))
        hp = list(range(8 * (1 - p), 8 * (1 - p) + 8))

        xb = x[b]
        xT = np.ascontiguousarray(np.concatenate([xb[my].T, xb[ot].T], axis=1))
        xres = np.ascontiguousarray(xb[my])
        cols = []
        for comp in range(3):
            for hh in hs:
                j = comp * 16 + hh
                cols.append(Wqkv[:, j * 128:(j + 1) * 128])
        lr_cols = Wlr[:, [comp * 16 + hh for comp in range(3) for hh in hs]]
        wcat = np.ascontiguousarray(np.concatenate(cols + [lr_cols], axis=1))
        wo = np.ascontiguousarray(np.concatenate(
            [Wo[hh * 128:(hh + 1) * 128] for hh in hs + hp], axis=0))
        w012t = np.ascontiguousarray(np.concatenate(
            [np.concatenate([w0[hh].T, w1[hh].T, w2[hh].T], axis=1) for hh in hs],
            axis=0))
        w012n = np.ascontiguousarray(np.concatenate(
            [np.concatenate([w0[hh], w1[hh], w2[hh]], axis=1) for hh in hs], axis=0))
        maps.append({
            "xT": xT.astype(np.float32), "xres": xres.astype(np.float32),
            "wcat": wcat.astype(np.float32), "wo": wo.astype(np.float32),
            "w012t": w012t.astype(np.float32), "w012n": w012n.astype(np.float32),
            "osc": osc_c, "fwT": fwT, "fbias": fbias, "identm": identm,
        })
    return maps


def kernel(**inputs):
    maps = stage_inputs(inputs)
    if _NC_CACHE[0] is None:
        _NC_CACHE[0] = build_program()
    nc = _NC_CACHE[0]
    res = run_bass_kernel_spmd(nc, maps, core_ids=list(range(N_CORES)))
    out = np.empty((B, L, D), dtype=np.float32)
    for c in range(N_CORES):
        b, p = divmod(c, 2)
        out[b, p * 1024:(p + 1) * 1024] = res.results[c]["out"]
    if DEBUG:
        kernel.debug_results = res.results
    return out
